# version 02
# baseline (speedup 1.0000x reference)
import numpy as np
import ml_dtypes
from contextlib import ExitStack

import concourse.bass as bass
import concourse.mybir as mybir
from concourse.bass_utils import run_bass_kernel_spmd

F32 = mybir.dt.float32
BF16 = mybir.dt.bfloat16
AF = mybir.ActivationFunctionType
ALU = mybir.AluOpType

T = 4096
D = 1024
NCH = 8
NQT = 8
QW = 512
NTC = 32
EPS = 1e-5

ENGS = ("tensor", "vector", "scalar", "gpsimd", "sync")
import os as _os
P3STAGE = int(_os.environ.get("P3STAGE", "9"))
DEBUG = False
MODE = "multi"


class Sem:
    __slots__ = ("h", "n")

    def __init__(self, h):
        self.h = h
        self.n = 0


class Rec:

    def __init__(self, nc, es):
        self.nc = nc
        self.es = es
        self.q = {e: [] for e in ENGS}
        self.tick = {e: Sem(es.enter_context(nc.semaphore(f"t_{e}"))) for e in ENGS}
        self.dtick = {e: Sem(es.enter_context(nc.semaphore(f"d_{e}"))) for e in
                      ("sync", "gpsimd", "scalar", "vector")}

    def sem(self, name=None):
        return Sem(self.es.enter_context(self.nc.semaphore(name)))

    def wait(self, eng, sem, val):
        if val > 0:
            self.q[eng].append(lambda e, _s=sem.h, _v=val: e.wait_ge(_s, _v))

    def waits(self, eng, *pairs):
        for sem, val in pairs:
            self.wait(eng, sem, val)

    def op(self, eng, f, tick=False):
        if tick:
            s = self.tick[eng]
            s.n += 1
            self.q[eng].append(lambda e, _f=f, _s=s.h: _f(e).then_inc(_s, 1))
            return s.n
        self.q[eng].append(f)
        return None

    def dma(self, eng, out, in_, **kw):
        s = self.dtick[eng]
        s.n += 16
        self.q[eng].append(
            lambda e, _o=out, _i=in_, _s=s.h, _k=kw:
            e.dma_start(out=_o, in_=_i, **_k).then_inc(_s, 16))
        return s.n

    def t(self, eng):
        return self.tick[eng].n

    def emit(self, block):
        for eng in ENGS:
            def run(e, _q=tuple(self.q[eng])):
                for f in _q:
                    f(e)
            getattr(block, eng)(run)



WEIGHT_SPECS = {
    "x0T": (NCH, 128, T),
    "wq": (NCH, 128, D), "wk": (NCH, 128, D), "wv": (NCH, 128, D),
    "lnq_s": (128, NCH), "lnk_s": (128, NCH), "lnv_s": (128, NCH),
    "lnq_b": (128, NCH), "lnk_b": (128, NCH), "lnv_b": (128, NCH),
    "bq": (128, NCH), "bk": (128, NCH), "bv_row": (1, D),
    "u1": (NCH, 128, 256),
    "wo": (NCH, 128, D), "bo": (128, NCH),
    "w1": (NCH, 128, 2 * D), "lnm_s": (128, NCH), "lnm_b": (128, NCH),
    "b1": (128, 2 * NCH),
    "w2": (2 * NCH, 128, D), "b2": (128, NCH), "b1f": (128, 4 * NCH),
    "v1": (2, 128, 4 * D), "b1f_row": (1, 4 * D),
    "u2": (4 * NCH, 128, 256), "v2": (2, 128, D), "b2f": (128, NCH),
    "phiT": (NCH, 128, 2 * D), "phib_row": (1, 2 * D),
    "wqT1": (NCH, 128, D), "wkT1": (NCH, 128, D), "wvT1": (NCH, 128, D),
    "wg1T": (NCH, 128, 128), "wg2T": (1, 128, D), "gnw_row": (1, D),
    "woT1": (NCH, 128, D),
    "ew": (8, NCH, 128, D),
    "eb4": (2, 4, D),
    "sel4c": (4, 128),
    "selx": (16, 128, 256),
    "identc": (128, 128),
    "triuc": (64, 64),
    "maskc": (64, 1),
}

SCRATCH_SPECS = {
    "xn0f": ((NCH, 128, T), F32),
    "xn0b": ((NCH, 128, T), BF16),
    "qb": ((NCH, 128, T), BF16),
    "kb": ((NCH, 128, T), BF16),
    "vp": ((NTC, 128, 16 * 65), BF16),
    "pfb": ((2, 128, T), BF16),
    "hhf": ((NCH, 128, T), F32),
    "x1f": ((NCH, 128, T), F32),
    "xn1f": ((NCH, 128, T), BF16),
    "xn1t": ((NTC, 128, D), BF16),
    "dtok": ((NTC, 128, 2 * D), BF16),
    "cfm": ((16, 128, T), BF16),
    "xpart": ((16, 128, D), F32),
    "xred": ((16, 128, D), F32),
    "ypart": ((2, 128, D), F32),
    "ytld": ((16, 128, D), F32),
    "otok": ((NTC, 128, D), BF16),
    "opfm": ((NCH, 128, T), BF16),
}

PHASE_IO = {
    "p1": ((), ("xn0f", "xn0b", "qb", "kb", "vp", "pfb")),
    "p2": (("qb", "kb", "vp", "xn0f"), ("hhf",)),
    "p3": (("hhf", "pfb"), ("x1f", "xn1f", "xn1t")),
    "p4a": (("xn1f", "xn1t"), ("dtok", "cfm", "xpart", "xred")),
    "p4b": (("xn1f",), ("otok",)),
    "p4c": (("xn1f", "otok"), ("opfm",)),
    "p4d": (("xred",), ("ypart", "ytld")),
    "p4e": (("x1f", "opfm", "cfm", "ytld"), ()),
}


def declare_drams(nc, phases, test_mode):
    d = {}
    for name, shape in WEIGHT_SPECS.items():
        dt = BF16 if name == "identc" else F32
        d[name] = nc.dram_tensor(name, list(shape), dt, kind="ExternalInput").ap()
    reads, writes = set(), set()
    for ph in phases:
        r, w = PHASE_IO[ph]
        reads |= set(r)
        writes |= set(w)
    for name in ("xpart_i", "xred_i", "ypart_i", "ytld_i"):
        base = name[:-2]
        shape, dt = SCRATCH_SPECS[base]
        d[name] = nc.dram_tensor(name, list(shape), dt, kind="Internal").ap()
    for name, (shape, dt) in SCRATCH_SPECS.items():
        if test_mode and name in writes:
            kind = "ExternalOutput"
        elif test_mode and name in reads:
            kind = "ExternalInput"
        else:
            kind = "Internal"
        d[name] = nc.dram_tensor(name, list(shape), dt, kind=kind).ap()
    d["outT"] = nc.dram_tensor("outT", [NCH, 128, T], F32, kind="ExternalOutput").ap()
    if test_mode:
        d["attdbg"] = nc.dram_tensor("attdbg", [NQT, 128, NCH * QW], BF16,
                                     kind="ExternalOutput").ap()
        d["xsldbg"] = nc.dram_tensor("xsldbg", [128, 2 * D], F32,
                                     kind="ExternalOutput").ap()
        d["xhdbg"] = nc.dram_tensor("xhdbg", [2, 128, D], BF16,
                                    kind="ExternalOutput").ap()
        d["attu"] = nc.dram_tensor("attu", [NQT, 16, 65, QW], F32,
                                   kind="ExternalOutput").ap()
    return d



class Ctx:
    def __init__(self, nc, es, rec, dram, test_mode=False):
        self.nc, self.es, self.rec, self.dram = nc, es, rec, dram
        self.test_mode = test_mode
        S = lambda n, s, dt: es.enter_context(nc.sbuf_tensor(n, s, dt)).ap()
        self.ones_b = S("c_ones_b", [1, 128], BF16)
        self.ones_f = S("c_ones_f", [1, 128], F32)
        self.onec_b = S("c_onec_b", [128, 1], BF16)
        self.cpool = S("c_pool", [128, 4], F32)
        self.ident_b = S("c_ident", [128, 128], BF16)
        self.triu_f = S("c_triu", [64, 64], F32)
        rec.dma("sync", self.ident_b, dram["identc"])
        rec.dma("sync", self.triu_f, dram["triuc"])
        rec.op("vector", lambda e: e.memset(self.ones_b, 1.0))
        rec.op("vector", lambda e: e.memset(self.onec_b, 1.0))
        rec.op("vector", lambda e: e.memset(self.cpool[:, 0:1], EPS))
        rec.op("vector", lambda e: e.memset(self.cpool[:, 1:2], -3.0))
        self.t_const = rec.op("vector", lambda e: e.memset(self.ones_f, 1.0), tick=True)
        self.d_const = rec.dtick["sync"].n


def ln_feature(ctx, W, chunks_bf, sq, rows, ps1, ps2, ps_bc, mu_b, rs_b,
               wsrc_t, prev_marks):
    rec = ctx.rec
    mu = rows[:, 0:W]
    va = rows[:, W:2 * W]
    m2 = rows[:, 2 * W:3 * W]
    pm = prev_marks or {}
    rec.waits("scalar", *wsrc_t)
    rec.wait("scalar", rec.tick["tensor"], pm.get("sq_read", 0))
    t_sq = None
    for k in range(NCH):
        t_sq = rec.op("scalar", lambda e, _i=chunks_bf[k], _o=sq[:, k * W:(k + 1) * W]:
                      e.activation(_o, _i, AF.Square), tick=(k == NCH - 1))
    rec.waits("tensor", *wsrc_t)
    rec.wait("tensor", rec.tick["scalar"], t_sq)
    rec.wait("tensor", rec.tick["vector"], pm.get("ps_read", 0))
    for k in range(NCH):
        rec.op("tensor", lambda e, _r=chunks_bf[k], _s=(k == 0), _e=(k == NCH - 1):
               e.matmul(ps1[0:1, :], ctx.onec_b, _r, start=_s, stop=_e))
    t_cs = None
    for k in range(NCH):
        t_cs = rec.op("tensor", lambda e, _r=sq[:, k * W:(k + 1) * W],
                      _s=(k == 0), _e=(k == NCH - 1):
                      e.matmul(ps2[0:1, :], ctx.onec_b, _r, start=_s, stop=_e),
                      tick=(k == NCH - 1))
    sq_read = t_cs
    rec.wait("vector", rec.tick["tensor"], t_cs)
    rec.wait("vector", rec.tick["scalar"], pm.get("rows_read_a", 0))
    rec.wait("vector", rec.tick["tensor"], pm.get("rows_read_t", 0))
    rec.op("vector", lambda e: e.tensor_scalar(mu, ps1[0:1, :], 1.0 / D, None, ALU.mult))
    rec.op("vector", lambda e: e.tensor_scalar(va, ps2[0:1, :], 1.0 / D, None, ALU.mult))
    rec.op("vector", lambda e: e.tensor_mul(m2, mu, mu))
    t_var = rec.op("vector", lambda e: e.tensor_sub(va, va, m2), tick=True)
    ps_read = t_var
    rec.wait("scalar", rec.tick["vector"], t_var)
    rec.op("scalar", lambda e: e.activation(va, va, AF.Ln, bias=ctx.cpool[0:1, 0:1]))
    t_rs = rec.op("scalar", lambda e: e.activation(va, va, AF.Exp, scale=-0.5), tick=True)
    rec.wait("tensor", rec.tick["scalar"], t_rs)
    rec.wait("tensor", rec.tick["vector"], pm.get("bc_read", 0))
    t_b1 = rec.op("tensor", lambda e: e.matmul(ps_bc[:, 0:W], ctx.ones_f[0:1, :], mu,
                                               start=True, stop=True), tick=True)
    rec.wait("vector", rec.tick["tensor"], t_b1)
    t_e1 = rec.op("vector", lambda e: e.tensor_copy(mu_b, ps_bc[:, 0:W]), tick=True)
    rec.wait("tensor", rec.tick["vector"], t_e1)
    t_b2 = rec.op("tensor", lambda e: e.matmul(ps_bc[:, 0:W], ctx.ones_f[0:1, :], va,
                                               start=True, stop=True), tick=True)
    rec.wait("vector", rec.tick["tensor"], t_b2)
    t_ev = rec.op("vector", lambda e: e.tensor_copy(rs_b, ps_bc[:, 0:W]), tick=True)
    return dict(t_ev=t_ev, sq_read=sq_read, ps_read=ps_read, t_sq=t_sq,
                rows_read_a=t_rs, rows_read_t=t_b2, bc_read=t_ev)



def phase1(ctx):
    nc, rec, dram = ctx.nc, ctx.rec, ctx.dram
    ph = ExitStack()
    S = lambda n, s, dt: ph.enter_context(nc.sbuf_tensor(n, s, dt)).ap()
    P = lambda n: ph.enter_context(nc.psum_tensor(n, [128, QW], F32)).ap()
    TT, VT, AT = rec.tick["tensor"], rec.tick["vector"], rec.tick["scalar"]
    DS, DG = rec.dtick["sync"], rec.dtick["gpsimd"]

    wq_s = S("p1_wq", [128, NCH * D], BF16)
    wk_s = S("p1_wk", [128, NCH * D], BF16)
    wv_s = S("p1_wv", [128, NCH * D], BF16)
    u1_s = S("p1_u1", [128, NCH * 256], BF16)
    wstg = S("p1_wstg", [128, 2 * D], F32)
    svec = S("p1_svec", [128, 8 * NCH], F32)
    svecb = S("p1_svecb", [128, 3 * NCH], BF16)
    bqp = S("p1_bqp", [128, 2 * NCH], F32)
    bvrow = S("p1_bvrow", [1, 2 * D], F32)
    vinit = S("p1_vinit", [128, 16 * 65], F32)
    x0f = S("p1_x0f", [128, 2 * NCH * QW], F32)
    x0b = S("p1_x0b", [128, NCH * QW], BF16)
    sq = S("p1_sq", [128, NCH * QW], BF16)
    xn0f_t = S("p1_xn0f", [128, NCH * QW], F32)
    xn0b_t = S("p1_xn0b", [128, NCH * QW], BF16)
    xnh_t = S("p1_xnh", [128, NCH * QW], BF16)
    tt = S("p1_tt", [128, QW], F32)
    rows = S("p1_rows", [1, 3 * QW], F32)
    mu_b = S("p1_mu_b", [128, QW], F32)
    rs_b = S("p1_rs_b", [128, QW], F32)
    qstg = S("p1_qstg", [128, NCH * QW], BF16)
    kstg = S("p1_kstg", [128, NCH * QW], BF16)
    vstg = S("p1_vstg", [128, 2 * 16 * 65], BF16)
    pstg = S("p1_pstg", [128, 2 * QW], BF16)

    ps1, ps2 = P("p1_s1"), P("p1_s2")
    ps_bc = P("p1_bc")
    ps_m = [P("p1_m0"), P("p1_m1")]
    ps_v = P("p1_v")
    ps_x = P("p1_x")

    for i, nm in enumerate(["lnq_s", "lnk_s", "lnv_s", "lnq_b", "lnk_b", "lnv_b",
                            "bq", "bk"]):
        rec.dma("sync", svec[:, i * NCH:(i + 1) * NCH], dram[nm])
    rec.dma("sync", bvrow[:, 0:D], dram["bv_row"])
    d_vecs = DS.n
    rec.wait("vector", DS, d_vecs)
    t_svecb = None
    for j in range(3):
        t_svecb = rec.op("vector", lambda e, _j=j: e.tensor_copy(
            svecb[:, _j * NCH:(_j + 1) * NCH], svec[:, (3 + _j) * NCH:(4 + _j) * NCH]),
            tick=(j == 2))

    li = 0
    t_scale = {}
    t_cast = {}
    m_wub_pe = {0: 0, 1: 0}
    t_bmm = None
    wub = S("p1_wub", [128, 2 * D], BF16)
    rec.wait("tensor", VT, ctx.t_const)
    first_bias_mm = [True]
    for wname, wdst, scol in [("wq", wq_s, 0), ("wk", wk_s, 1), ("wv", wv_s, 2)]:
        for k in range(NCH):
            par = li % 2
            buf = wstg[:, par * D:(par + 1) * D]
            if li >= 2:
                rec.wait("sync", VT, t_cast[li - 2])
            d_w = rec.dma("sync", buf, dram[wname][k])
            rec.wait("vector", DS, d_w)
            t_scale[li] = rec.op(
                "vector", lambda e, _b=buf, _d=wdst[:, k * D:(k + 1) * D],
                _s=svec[:, scol * NCH + k:scol * NCH + k + 1]:
                e.tensor_scalar(_d, _b, _s, None, ALU.mult), tick=True)
            rec.wait("vector", TT, m_wub_pe[par])
            t_cast[li] = rec.op(
                "vector", lambda e, _b=buf, _d=wub[:, par * D:(par + 1) * D]:
                e.tensor_copy(_d, _b), tick=True)
            rec.wait("tensor", VT, t_cast[li])
            if scol < 2:
                for m in range(NCH):
                    st_flag = first_bias_mm[0]
                    first_bias_mm[0] = False
                    t_bmm = rec.op(
                        "tensor",
                        lambda e, _l=wub[:, par * D + m * 128:par * D + (m + 1) * 128],
                        _r=svecb[:, scol * NCH + k:scol * NCH + k + 1],
                        _o=ps_x[:, scol * NCH + m:scol * NCH + m + 1],
                        _s=st_flag, _e=(k == NCH - 1):
                        e.matmul(_o, _l, _r, start=_s, stop=_e,
                                 skip_group_check=True), tick=(m == NCH - 1))
            else:
                for n in range(2):
                    t_bmm = rec.op(
                        "tensor",
                        lambda e, _l=svecb[:, 2 * NCH + k:2 * NCH + k + 1],
                        _r=wub[:, par * D + n * QW:par * D + (n + 1) * QW],
                        _o=(ps_v if n == 0 else ps_m[0])[0:1, :],
                        _s=(k == 0), _e=(k == NCH - 1):
                        e.matmul(_o, _l, _r, start=_s, stop=_e,
                                 skip_group_check=True), tick=(n == 1))
            m_wub_pe[par] = t_bmm
            li += 1
    t_wprep = t_scale[li - 1]
    d_u1 = rec.dma("gpsimd", u1_s[:].rearrange("p (c n) -> p c n", c=NCH),
                   dram["u1"].rearrange("c p n -> p c n"))

    rec.waits("scalar", (TT, t_bmm), (DS, d_vecs))
    t_ev = 0
    for j in range(2):
        for m in range(NCH):
            t_ev = rec.op("scalar",
                          lambda e, _o=bqp[:, j * NCH + m:j * NCH + m + 1],
                          _i=ps_x[:, j * NCH + m:j * NCH + m + 1],
                          _b=svec[:, (6 + j) * NCH + m:(6 + j) * NCH + m + 1]:
                          e.activation(_o, _i, AF.Identity, bias=_b), tick=True)
    t_bqp = t_ev
    rec.wait("vector", TT, t_bmm)
    t_vev = None
    for n in range(2):
        t_vev = rec.op("vector", lambda e, _n=n,
                       _i=(ps_v if n == 0 else ps_m[0])[0:1, :]:
                       e.tensor_add(bvrow[:, D + _n * QW:D + (_n + 1) * QW], _i,
                                    bvrow[:, _n * QW:(_n + 1) * QW]), tick=True)
    rec.op("vector", lambda e: e.memset(
        vinit[:].rearrange("p (h c) -> p h c", h=16)[:, :, 64:65], 1.0))
    t_ones = rec.op("vector", lambda e: e.memset(
        vstg[:].rearrange("p (b h c) -> p b h c", b=2, h=16)[:, :, :, 64:65], 1.0),
        tick=True)
    rec.wait("tensor", VT, t_ones)
    t_vinit = 0
    for n in range(2):
        t_mm = rec.op("tensor", lambda e, _n=n: e.matmul(
            ps_bc[:, 0:QW], ctx.ones_f[0:1, :],
            bvrow[0:1, D + _n * QW:D + (_n + 1) * QW], start=True, stop=True),
            tick=True)
        rec.wait("vector", TT, t_mm)
        t_vinit = rec.op("vector", lambda e, _n=n: e.tensor_copy(
            vinit[:].rearrange("p (h c) -> p h c", h=16)[:, _n * 8:(_n + 1) * 8, 0:64],
            ps_bc[:, 0:QW].rearrange("p (h c) -> p h c", h=8)), tick=True)
        rec.wait("tensor", VT, t_vinit)

    marks0 = dict(sq_read=0, ps_read=t_vinit, rows_read_a=0, rows_read_t=0,
                  bc_read=t_vinit)
    m_x0_read = {}
    m_x0b_pe = 0
    m_x0b_sq = 0
    m_store = {}
    m_bank = {0: 0, 1: 0}
    m_psv = t_vinit
    m_vstg = {}
    t_apply = 0
    for qt in range(NQT):
        par = qt % 2
        x0qt = x0f[:, par * NCH * QW:(par + 1) * NCH * QW]
        x0c = [x0qt[:, k * QW:(k + 1) * QW] for k in range(NCH)]
        rec.wait("sync", VT, m_x0_read.get(par, 0))
        d_x0 = rec.dma("sync", x0qt.rearrange("p (c w) -> p c w", c=NCH),
                       dram["x0T"][:, :, qt * QW:(qt + 1) * QW].rearrange("c p w -> p c w"))
        rec.wait("vector", DS, d_x0)
        rec.wait("vector", TT, m_x0b_pe)
        rec.wait("vector", AT, m_x0b_sq)
        t_cast = None
        for k in range(NCH):
            t_cast = rec.op("vector", lambda e, _k=k, _s=x0c[k]:
                            e.tensor_copy(x0b[:, _k * QW:(_k + 1) * QW], _s),
                            tick=(k == NCH - 1))
        x0bc = [x0b[:, k * QW:(k + 1) * QW] for k in range(NCH)]
        marks0 = ln_feature(ctx, QW, x0bc, sq, rows, ps1, ps2, ps_bc, mu_b, rs_b,
                            [(VT, t_cast)], marks0)
        m_x0b_pe = marks0["sq_read"]
        m_x0b_sq = marks0["t_sq"]
        rec.wait("vector", DS, m_store.get("xn0", 0))
        rec.wait("vector", TT, m_store.get("xn0b_pe", 0))
        for k in range(NCH):
            rec.op("vector", lambda e, _x=x0c[k]: e.tensor_sub(tt, _x, mu_b))
            rec.op("vector", lambda e, _k=k: e.tensor_mul(
                xn0f_t[:, _k * QW:(_k + 1) * QW], tt, rs_b))
            t_apply = rec.op("vector", lambda e, _k=k: e.tensor_copy(
                xn0b_t[:, _k * QW:(_k + 1) * QW], xn0f_t[:, _k * QW:(_k + 1) * QW]),
                tick=(k == NCH - 1))
        m_x0_read[par] = t_apply
        rec.wait("sync", VT, t_apply)
        m_store["xn0"] = rec.dma(
            "sync", dram["xn0f"][:, :, qt * QW:(qt + 1) * QW].rearrange("c p w -> p c w"),
            xn0f_t[:].rearrange("p (c w) -> p c w", c=NCH))
        m_store["xn0"] = rec.dma(
            "sync", dram["xn0b"][:, :, qt * QW:(qt + 1) * QW].rearrange("c p w -> p c w"),
            xn0b_t[:].rearrange("p (c w) -> p c w", c=NCH))
        xn0bc = [xn0b_t[:, k * QW:(k + 1) * QW] for k in range(NCH)]
        marks0 = ln_feature(ctx, QW, xn0bc, sq, rows, ps1, ps2, ps_bc, mu_b, rs_b,
                            [(VT, t_apply)], marks0)
        rec.wait("vector", TT, m_store.get("xnh_pe", 0))
        t_xnh = None
        for k in range(NCH):
            rec.op("vector", lambda e, _k=k: e.tensor_sub(
                tt, xn0f_t[:, _k * QW:(_k + 1) * QW], mu_b))
            t_xnh = rec.op("vector", lambda e, _k=k: e.tensor_mul(
                xnh_t[:, _k * QW:(_k + 1) * QW], tt, rs_b), tick=(k == NCH - 1))

        rec.wait("tensor", VT, t_xnh)
        rec.wait("tensor", AT, t_bqp)
        for pi, (wdst, stg, bcoff, dname) in enumerate(
                [(wq_s, qstg, 0, "qb"), (wk_s, kstg, NCH, "kb")]):
            rec.wait("scalar", DS, m_store.get(dname, 0))
            for m in range(NCH):
                pb = ps_m[m % 2]
                rec.wait("tensor", AT, m_bank[m % 2])
                t_mm = None
                for k in range(NCH):
                    t_mm = rec.op(
                        "tensor",
                        lambda e, _l=wdst[:, k * D + m * 128:k * D + (m + 1) * 128],
                        _r=xnh_t[:, k * QW:(k + 1) * QW], _p=pb,
                        _s=(k == 0), _e=(k == NCH - 1):
                        e.matmul(_p, _l, _r, start=_s, stop=_e), tick=(k == NCH - 1))
                rec.wait("scalar", TT, t_mm)
                m_bank[m % 2] = rec.op(
                    "scalar", lambda e, _o=stg[:, m * QW:(m + 1) * QW], _p=pb,
                    _b=bqp[:, bcoff + m:bcoff + m + 1]:
                    e.activation(_o, _p, AF.Identity, bias=_b), tick=True)
            rec.wait("sync", AT, max(m_bank[0], m_bank[1]))
            m_store[dname] = rec.dma(
                "sync", dram[dname][:, :, qt * QW:(qt + 1) * QW].rearrange("c p w -> p c w"),
                stg[:].rearrange("p (c w) -> p c w", c=NCH))

        for tc in range(4):
            vpar = tc % 2
            vs = vstg[:, vpar * 16 * 65:(vpar + 1) * 16 * 65]
            for n in range(2):
                rec.wait("tensor", VT, m_psv)
                t_mm = None
                for k in range(NCH):
                    t_mm = rec.op(
                        "tensor",
                        lambda e, _l=xnh_t[:, k * QW + tc * 128:k * QW + tc * 128 + 128],
                        _r=wv_s[:, k * D + n * QW:k * D + (n + 1) * QW],
                        _s=(k == 0), _e=(k == NCH - 1):
                        e.matmul(ps_v, _l, _r, start=_s, stop=_e), tick=(k == NCH - 1))
                rec.wait("vector", TT, t_mm)
                if n == 0:
                    rec.wait("vector", DS, m_vstg.get(vpar, 0))
                m_psv = rec.op("vector", lambda e, _n=n, _vs=vs: e.scalar_tensor_tensor(
                    _vs.rearrange("p (h c) -> p h c", h=16)[:, _n * 8:(_n + 1) * 8, 0:64],
                    ps_v.rearrange("p (h c) -> p h c", h=8), 1.0,
                    vinit.rearrange("p (h c) -> p h c", h=16)[:, _n * 8:(_n + 1) * 8, 0:64],
                    ALU.mult, ALU.add), tick=True)
            rec.wait("sync", VT, m_psv)
            m_vstg[vpar] = rec.dma("sync", dram["vp"][qt * 4 + tc], vs)
        m_store["xnh_pe"] = rec.t("tensor")

        rec.wait("tensor", DG, d_u1)
        rec.wait("scalar", DS, m_store.get("pfb", 0))
        for m2 in range(2):
            pb = ps_m[m2 % 2]
            rec.wait("tensor", AT, m_bank[m2 % 2])
            t_mm = None
            for k in range(NCH):
                t_mm = rec.op(
                    "tensor",
                    lambda e, _l=u1_s[:, k * 256 + m2 * 128:k * 256 + (m2 + 1) * 128],
                    _r=xn0b_t[:, k * QW:(k + 1) * QW], _p=pb,
                    _s=(k == 0), _e=(k == NCH - 1):
                    e.matmul(_p, _l, _r, start=_s, stop=_e), tick=(k == NCH - 1))
            rec.wait("scalar", TT, t_mm)
            m_bank[m2 % 2] = rec.op(
                "scalar", lambda e, _o=pstg[:, m2 * QW:(m2 + 1) * QW], _p=pb:
                e.activation(_o, _p, AF.Copy), tick=True)
        rec.wait("sync", AT, max(m_bank[0], m_bank[1]))
        m_store["pfb"] = rec.dma(
            "sync", dram["pfb"][:, :, qt * QW:(qt + 1) * QW].rearrange("c p w -> p c w"),
            pstg[:].rearrange("p (c w) -> p c w", c=2))
        m_store["xn0b_pe"] = rec.t("tensor")

    end = {eng: rec.t(eng) for eng in ENGS}
    dend = {q: rec.dtick[q].n for q in rec.dtick}
    ph.close()
    return end, dend



def phase2(ctx):
    nc, rec, dram = ctx.nc, ctx.rec, ctx.dram
    rec.wait("gpsimd", rec.dtick["sync"], rec.dtick["sync"].n)
    ph = ExitStack()
    S = lambda n, s, dt: ph.enter_context(nc.sbuf_tensor(n, s, dt)).ap()
    TT, VT, AT = rec.tick["tensor"], rec.tick["vector"], rec.tick["scalar"]
    DS, DG = rec.dtick["sync"], rec.dtick["gpsimd"]

    vp_s = S("p2_vp", [128, NTC * 16 * 65], BF16)
    wo_s = S("p2_wo", [128, NCH * D], BF16)
    bo_c = S("p2_bo", [128, NCH], F32)
    q_qt = S("p2_q", [128, NCH * QW], BF16)
    k_h = S("p2_k", [128, T], BF16)
    p_ring = S("p2_p", [128, 8 * 2 * QW], BF16)
    att_all = S("p2_att", [128, NCH * QW], BF16)
    xn0q = S("p2_xn0q", [128, NCH * QW], F32)
    hhstg = S("p2_hh", [128, NCH * QW], F32)
    rd_b = S("p2_rd", [1, QW], BF16)
    rd_f = S("p2_rdf", [1, QW], F32)
    rdb_s = S("p2_rdb", [64, QW], F32)
    dbg_s = S("p2_dbg", [64, QW], F32)
    mcol = S("p2_mcol", [64, 1], F32)

    PS = lambda n, w: ph.enter_context(nc.psum_tensor(n, [128, w], F32)).ap()
    ps_qk = [PS("p2_qk0", 2 * QW), PS("p2_qk1", 2 * QW)]
    ps_pv = PS("p2_pv", QW)
    ps_o = [PS("p2_o0", QW), PS("p2_o1", QW)]
    ps_rb = PS("p2_rb", QW)

    d_mask = rec.dma("sync", mcol, dram["maskc"])

    for tc in range(NTC):
        rec.dma("sync", vp_s[:, tc * 1040:(tc + 1) * 1040], dram["vp"][tc])
    d_vp = rec.dtick["sync"].n
    d_wo = rec.dma("gpsimd", wo_s[:].rearrange("p (c n) -> p c n", c=NCH),
                   dram["wo"].rearrange("c p n -> p c n"))
    d_bo = rec.dma("sync", bo_c, dram["bo"])

    m_qk = {0: 0, 1: 0}
    m_pring = [0] * 8
    m_kh = {0: 0, 1: 0}
    m_ps_pv = 0
    m_att_pe = 0
    m_q_pe = 0
    m_obank = {0: 0, 1: 0}
    m_hh_st = 0
    m_rb = 0
    gg = 0
    for qt in range(NQT):
        rec.wait("sync", TT, m_q_pe)
        d_q = rec.dma("sync", q_qt.rearrange("p (c w) -> p c w", c=NCH),
                      dram["qb"][:, :, qt * QW:(qt + 1) * QW].rearrange("c p w -> p c w"))
        rec.wait("sync", DS, m_hh_st)
        d_xn0 = rec.dma("sync", xn0q.rearrange("p (c w) -> p c w", c=NCH),
                        dram["xn0f"][:, :, qt * QW:(qt + 1) * QW].rearrange("c p w -> p c w"))
        for h in range(16):
            par = h % 2
            kbuf = k_h[par * 64:par * 64 + 64, :]
            rec.wait("sync", TT, m_kh[par])
            d_k = rec.dma("sync", kbuf, dram["kb"][h // 2, (h % 2) * 64:(h % 2) * 64 + 64, :])
            qh = q_qt[(h % 2) * 64:(h % 2) * 64 + 64, (h // 2) * QW:(h // 2 + 1) * QW]
            t_pv_last = None
            for g in range(16):
                reg = ps_qk[g % 2]
                rec.wait("tensor", AT, m_qk[g % 2])
                if g == 0:
                    rec.waits("tensor", (DS, d_k), (DS, d_q))
                t_qk = None
                for j in range(2):
                    kt = g * 2 + j
                    t_qk = rec.op(
                        "tensor",
                        lambda e, _l=kbuf[:, kt * 128:(kt + 1) * 128], _q=qh,
                        _p=reg[:, j * QW:(j + 1) * QW]:
                        e.matmul(_p, _l, _q, start=True, stop=True,
                                 skip_group_check=True), tick=(j == 1))
                t_exp_pre = t_qk
                if qt == 0 and g == 15:
                    rec.wait("vector", TT, t_qk)
                    rec.wait("vector", DS, d_mask)
                    t_msk = rec.op("vector", lambda e, _p=reg[64:128, 1 * QW:1 * QW + 1]:
                                   e.tensor_add(_p, _p, mcol), tick=True)
                    rec.wait("scalar", VT, t_msk)
                slot = gg % 8
                rec.wait("scalar", TT, t_exp_pre)
                rec.wait("scalar", TT, m_pring[slot])
                m_qk[g % 2] = rec.op(
                    "scalar", lambda e, _p=reg, _o=p_ring[:, slot * 2 * QW:(slot + 1) * 2 * QW]:
                    e.activation(_o, _p, AF.Exp, scale=1.0 / 32.0,
                                 bias=ctx.cpool[:, 1:2]), tick=True)
                rec.wait("tensor", AT, m_qk[g % 2])
                if g == 0:
                    rec.waits("tensor", (VT, m_ps_pv), (DS, d_vp))
                for j in range(2):
                    kt = g * 2 + j
                    t_pv_last = rec.op(
                        "tensor",
                        lambda e, _l=vp_s[:, kt * 1040 + h * 65:kt * 1040 + h * 65 + 65],
                        _r=p_ring[:, (slot * 2 + j) * QW:(slot * 2 + j + 1) * QW],
                        _st=(kt == 0), _sp=(kt == 31):
                        e.matmul(ps_pv[0:65, :], _l, _r, start=_st, stop=_sp,
                                 skip_group_check=True), tick=(j == 1))
                m_pring[slot] = t_pv_last
                gg += 1
            rec.wait("vector", TT, t_pv_last)
            rec.op("vector", lambda e: e.reciprocal(rd_f, ps_pv[64:65, :]))
            t_rd = rec.op("vector", lambda e: e.tensor_copy(rd_b, rd_f), tick=True)
            rec.wait("tensor", VT, t_rd)
            rec.wait("tensor", AT, m_rb)
            t_rb = rec.op("tensor", lambda e: e.matmul(
                ps_rb, ctx.ones_b[0:1, :], rd_b, start=True, stop=True), tick=True)
            rec.wait("scalar", TT, t_rb)
            m_rb = rec.op("scalar", lambda e: e.activation(rdb_s, ps_rb[0:64, :], AF.Copy),
                          tick=True)
            if "attu" in dram:
                rec.wait("scalar", TT, t_pv_last)
                t_dbg = rec.op("scalar", lambda e: e.copy(dbg_s, ps_pv[0:64, :]),
                               tick=True)
                rec.wait("sync", AT, t_dbg)
                rec.dma("sync", dram["attu"][qt, h, 0:64, :], dbg_s)
                rec.wait("sync", VT, t_rd)
                rec.dma("sync", dram["attu"][qt, h, 64:65, :], rd_f)
                m_rb = t_dbg
            rec.wait("vector", AT, m_rb)
            rec.wait("vector", TT, m_att_pe)
            m_ps_pv = rec.op("vector", lambda e, _o=att_all[(h % 2) * 64:(h % 2) * 64 + 64,
                                                            (h // 2) * QW:(h // 2 + 1) * QW]:
                             e.tensor_mul(_o, ps_pv[0:64, :], rdb_s), tick=True)
            m_kh[par] = rec.t("tensor")
        t_att = m_ps_pv
        rec.waits("tensor", (VT, t_att), (DG, d_wo))
        t_hh = 0
        for m in range(NCH):
            pb = ps_o[m % 2]
            rec.wait("tensor", AT, m_obank[m % 2])
            t_mm = None
            for k in range(NCH):
                t_mm = rec.op(
                    "tensor",
                    lambda e, _l=wo_s[:, k * D + m * 128:k * D + (m + 1) * 128],
                    _r=att_all[:, k * QW:(k + 1) * QW], _p=pb,
                    _s=(k == 0), _e=(k == NCH - 1):
                    e.matmul(_p, _l, _r, start=_s, stop=_e, skip_group_check=True),
                    tick=(k == NCH - 1))
            rec.waits("scalar", (TT, t_mm), (DS, d_bo))
            m_obank[m % 2] = rec.op(
                "scalar", lambda e, _o=hhstg[:, m * QW:(m + 1) * QW], _p=pb,
                _b=bo_c[:, m:m + 1]:
                e.activation(_o, _p, AF.Identity, bias=_b), tick=True)
            rec.waits("vector", (AT, m_obank[m % 2]), (DS, d_xn0))
            t_hh = rec.op("vector", lambda e, _m=m: e.tensor_add(
                hhstg[:, _m * QW:(_m + 1) * QW], hhstg[:, _m * QW:(_m + 1) * QW],
                xn0q[:, _m * QW:(_m + 1) * QW]), tick=True)
        m_att_pe = rec.t("tensor")
        if "attdbg" in dram:
            rec.wait("sync", VT, m_ps_pv)
            rec.dma("sync", dram["attdbg"][qt], att_all)
        rec.wait("sync", VT, t_hh)
        m_hh_st = rec.dma(
            "sync", dram["hhf"][:, :, qt * QW:(qt + 1) * QW].rearrange("c p w -> p c w"),
            hhstg[:].rearrange("p (c w) -> p c w", c=NCH))
        m_q_pe = rec.t("tensor")
    ph.close()



def phase3(ctx):
    nc, rec, dram = ctx.nc, ctx.rec, ctx.dram
    rec.wait("gpsimd", rec.dtick["sync"], rec.dtick["sync"].n)
    rec.wait("sync", rec.dtick["sync"], rec.dtick["sync"].n)
    ph = ExitStack()
    S = lambda n, s, dt: ph.enter_context(nc.sbuf_tensor(n, s, dt)).ap()
    P = lambda n: ph.enter_context(nc.psum_tensor(n, [128, QW], F32)).ap()
    TT, VT, AT = rec.tick["tensor"], rec.tick["vector"], rec.tick["scalar"]
    DS, DG = rec.dtick["sync"], rec.dtick["gpsimd"]

    w1_s = S("p3_w1", [128, NCH * 2 * D], BF16)
    w2_s = S("p3_w2", [128, 2 * NCH * D], BF16)
    u2_s = S("p3_u2", [128, 4 * NCH * 256], BF16)
    v1_s = S("p3_v1", [128, 2 * 4 * D], BF16)
    v2_s = S("p3_v2", [128, 2 * D], BF16)
    wstg = S("p3_wstg", [128, 2 * D], F32)
    svec = S("p3_svec", [128, 2 * NCH], F32)
    svecb = S("p3_svecb", [128, NCH], BF16)
    bvec = S("p3_bvec", [128, 8 * NCH], F32)
    b1p = S("p3_b1p", [128, 2 * NCH], F32)
    hhq = S("p3_hhq", [128, NCH * QW], F32)
    hhb = S("p3_hhb", [128, NCH * QW], BF16)
    pq = S("p3_pq", [128, 2 * QW], BF16)
    sq = S("p3_sq", [128, NCH * QW], BF16)
    rows = S("p3_rows", [1, 3 * QW], F32)
    mu_b = S("p3_mu_b", [128, QW], F32)
    rs_b = S("p3_rs_b", [128, QW], F32)
    tt = S("p3_tt", [128, QW], F32)
    x0t = S("p3_x0t", [128, QW], F32)
    yh = S("p3_yh", [128, NCH * QW], BF16)
    g1b = S("p3_g1b", [128, 2 * NCH * QW], BF16)
    hb = S("p3_hb", [128, 2 * QW], BF16)
    q2b = S("p3_q2b", [128, 2 * QW], BF16)
    xtstg = S("p3_xt", [128, 4 * D], BF16)

    ps1, ps2 = P("p3_s1"), P("p3_s2")
    ps_bc = P("p3_bc")
    ps_m = [P("p3_m0"), P("p3_m1")]
    ps_q = [P("p3_q0"), P("p3_q1")]
    ps_tr = ph.enter_context(nc.psum_tensor("p3_tr", [128, 1024], BF16)).ap()

    rec.dma("sync", svec[:, 0:NCH], dram["lnm_s"])
    rec.dma("sync", svec[:, NCH:2 * NCH], dram["lnm_b"])
    rec.dma("sync", bvec[:, 0:2 * NCH], dram["b1"])
    rec.dma("sync", bvec[:, 2 * NCH:3 * NCH], dram["b2"])
    rec.dma("sync", bvec[:, 3 * NCH:7 * NCH], dram["b1f"])
    d_vecs = rec.dma("sync", bvec[:, 7 * NCH:8 * NCH], dram["b2f"])
    rec.wait("vector", DS, d_vecs)
    rec.op("vector", lambda e: e.tensor_copy(svecb, svec[:, NCH:2 * NCH]), tick=True)
    wub3 = S("p3_wub", [128, 2 * D], BF16)
    t_scale = 0
    t_bmm = None
    rec.wait("tensor", VT, ctx.t_const)
    for k in range(NCH):
        if t_scale:
            rec.wait("sync", VT, t_scale)
        d_w = rec.dma("sync", wstg, dram["w1"][k])
        rec.wait("vector", DS, d_w)
        rec.op("vector", lambda e, _d=w1_s[:, k * 2 * D:(k + 1) * 2 * D],
               _s=svec[:, k:k + 1]:
               e.tensor_scalar(_d, wstg, _s, None, ALU.mult))
        if t_bmm:
            rec.wait("vector", TT, t_bmm)
        t_scale = rec.op("vector", lambda e: e.tensor_copy(wub3, wstg), tick=True)
        rec.waits("tensor", (VT, t_scale))
        for m in range(2 * NCH):
            t_bmm = rec.op(
                "tensor",
                lambda e, _l=wub3[:, m * 128:(m + 1) * 128],
                _r=svecb[:, k:k + 1],
                _o=ps_m[0][:, m:m + 1],
                _s=(k == 0 and m == 0), _e=(k == NCH - 1):
                e.matmul(_o, _l, _r, start=_s, stop=_e, skip_group_check=True),
                tick=(m == 2 * NCH - 1))
    t_wprep = t_scale
    d_w2 = rec.dma("gpsimd", w2_s[:].rearrange("p (c n) -> p c n", c=2 * NCH),
                   dram["w2"].rearrange("c p n -> p c n"))
    d_u2 = rec.dma("gpsimd", u2_s[:].rearrange("p (c n) -> p c n", c=4 * NCH),
                   dram["u2"].rearrange("c p n -> p c n"))
    d_v1 = rec.dma("gpsimd", v1_s[:].rearrange("p (c n) -> p c n", c=2),
                   dram["v1"].rearrange("c p n -> p c n"))
    d_v2 = rec.dma("gpsimd", v2_s[:].rearrange("p (c n) -> p c n", c=2),
                   dram["v2"].rearrange("c p n -> p c n"))
    d_wld = DG.n

    rec.waits("scalar", (TT, t_bmm), (DS, d_vecs))
    t_ev = 0
    for m in range(2 * NCH):
        t_ev = rec.op("scalar", lambda e, _o=b1p[:, m:m + 1], _b=bvec[:, m:m + 1],
                      _i=ps_m[0][:, m:m + 1]:
                      e.activation(_o, _i, AF.Identity, bias=_b), tick=True)
    t_b1p = t_ev

    marks = dict(sq_read=0, ps_read=0, rows_read_a=0, rows_read_t=0, bc_read=0,
                 t_sq=0)
    m_store = {}
    m_bank = {0: 0, 1: 0}
    m_bank_eng = {0: "scalar", 1: "scalar"}
    m_qbank = 0
    m_g1_pe = 0
    m_yh_pe = 0
    m_pq_pe = 0
    m_hb_pe = {0: 0, 1: 0}
    m_q2_pe = 0
    m_xt_pe = 0
    first = True
    for qt in range(NQT):
        rec.wait("sync", VT, m_store.get("hhq_rd", 0))
        d_hh = rec.dma("sync", hhq.rearrange("p (c w) -> p c w", c=NCH),
                       dram["hhf"][:, :, qt * QW:(qt + 1) * QW].rearrange("c p w -> p c w"))
        rec.wait("sync", TT, m_pq_pe)
        d_pq = rec.dma("sync", pq.rearrange("p (c w) -> p c w", c=2),
                       dram["pfb"][:, :, qt * QW:(qt + 1) * QW].rearrange("c p w -> p c w"))
        rec.waits("vector", (DS, d_hh), (TT, marks["sq_read"]), (AT, marks["t_sq"]))
        t_cast = None
        for k in range(NCH):
            t_cast = rec.op("vector", lambda e, _k=k: e.tensor_copy(
                hhb[:, _k * QW:(_k + 1) * QW], hhq[:, _k * QW:(_k + 1) * QW]),
                tick=(k == NCH - 1))
        hhbc = [hhb[:, k * QW:(k + 1) * QW] for k in range(NCH)]
        if first:
            rec.wait("tensor", AT, t_b1p)
            first = False
        marks = ln_feature(ctx, QW, hhbc, sq, rows, ps1, ps2, ps_bc, mu_b, rs_b,
                           [(VT, t_cast)], marks)
        rec.wait("vector", TT, m_yh_pe)
        t_yh = None
        for k in range(NCH):
            rec.op("vector", lambda e, _k=k: e.tensor_sub(
                tt, hhq[:, _k * QW:(_k + 1) * QW], mu_b))
            t_yh = rec.op("vector", lambda e, _k=k: e.tensor_mul(
                yh[:, _k * QW:(_k + 1) * QW], tt, rs_b), tick=(k == NCH - 1))

        if P3STAGE < 1:
            continue
        rec.wait("tensor", VT, t_yh)
        rec.wait("tensor", AT, t_b1p)
        for m in range(2 * NCH):
            pb = ps_m[m % 2]
            rec.wait("tensor", m_bank_eng[m % 2] == "scalar" and AT or VT, m_bank[m % 2])
            t_mm = None
            for k in range(NCH):
                t_mm = rec.op(
                    "tensor",
                    lambda e, _l=w1_s[:, k * 2 * D + m * 128:k * 2 * D + (m + 1) * 128],
                    _r=yh[:, k * QW:(k + 1) * QW], _p=pb,
                    _s=(k == 0), _e=(k == NCH - 1):
                    e.matmul(_p, _l, _r, start=_s, stop=_e), tick=(k == NCH - 1))
            rec.wait("scalar", TT, t_mm)
            if m == 0:
                rec.wait("scalar", TT, m_g1_pe)
            m_bank[m % 2] = rec.op(
                "scalar", lambda e, _o=g1b[:, m * QW:(m + 1) * QW], _p=pb,
                _b=b1p[:, m:m + 1]:
                e.activation(_o, _p, AF.Gelu_apprx_tanh, bias=_b), tick=True)
            m_bank_eng[m % 2] = "scalar"
        t_g1 = m_bank[1]
        if P3STAGE < 2:
            continue
        rec.waits("tensor", (AT, t_g1), (DG, d_w2))
        t_x1m = 0
        for m in range(NCH):
            pb = ps_m[m % 2]
            rec.wait("tensor", m_bank_eng[m % 2] == "scalar" and AT or VT, m_bank[m % 2])
            t_mm = None
            for k in range(2 * NCH):
                t_mm = rec.op(
                    "tensor",
                    lambda e, _l=w2_s[:, k * D + m * 128:k * D + (m + 1) * 128],
                    _r=g1b[:, k * QW:(k + 1) * QW], _p=pb,
                    _s=(k == 0), _e=(k == 2 * NCH - 1):
                    e.matmul(_p, _l, _r, start=_s, stop=_e), tick=(k == 2 * NCH - 1))
            rec.wait("vector", TT, t_mm)
            t_x1m = rec.op(
                "vector", lambda e, _o=hhq[:, m * QW:(m + 1) * QW], _p=pb,
                _b=bvec[:, 2 * NCH + m:2 * NCH + m + 1]:
                e.scalar_tensor_tensor(_o, _p, _b, _o, ALU.add, ALU.add), tick=True)
            m_bank[m % 2] = t_x1m
            m_bank_eng[m % 2] = "vector"
        m_g1_pe = rec.t("tensor")

        if P3STAGE < 3:
            continue
        rec.waits("tensor", (DS, d_pq), (DG, d_wld))
        rec.wait("tensor", AT, m_qbank)
        rec.wait("tensor", TT, m_q2_pe)
        for fm in range(4 * NCH):
            pb = ps_m[fm % 2]
            rec.wait("tensor", m_bank_eng[fm % 2] == "scalar" and AT or VT, m_bank[fm % 2])
            t_mm = None
            for kc in range(2):
                t_mm = rec.op(
                    "tensor",
                    lambda e, _l=v1_s[:, kc * 4 * D + fm * 128:kc * 4 * D + (fm + 1) * 128],
                    _r=pq[:, kc * QW:(kc + 1) * QW], _p=pb,
                    _s=(kc == 0), _e=(kc == 1):
                    e.matmul(_p, _l, _r, start=_s, stop=_e, skip_group_check=True),
                    tick=(kc == 1))
            rec.wait("scalar", TT, t_mm)
            rec.wait("scalar", TT, m_hb_pe[fm % 2])
            m_bank[fm % 2] = rec.op(
                "scalar", lambda e, _o=hb[:, (fm % 2) * QW:((fm % 2) + 1) * QW], _p=pb,
                _b=bvec[:, 3 * NCH + fm:3 * NCH + fm + 1]:
                e.activation(_o, _p, AF.Gelu_apprx_tanh, bias=_b), tick=True)
            m_bank_eng[fm % 2] = "scalar"
            rec.wait("tensor", AT, m_bank[fm % 2])
            for m2 in range(2):
                m_hb_pe[fm % 2] = rec.op(
                    "tensor",
                    lambda e, _l=u2_s[:, fm * 256 + m2 * 128:fm * 256 + (m2 + 1) * 128],
                    _r=hb[:, (fm % 2) * QW:((fm % 2) + 1) * QW], _p=ps_q[m2],
                    _s=(fm == 0), _e=(fm == 4 * NCH - 1):
                    e.matmul(_p, _l, _r, start=_s, stop=_e, skip_group_check=True),
                    tick=(m2 == 1))
        t_q2 = m_hb_pe[(4 * NCH - 1) % 2]
        if P3STAGE < 4:
            continue
        rec.wait("scalar", TT, t_q2)
        rec.wait("scalar", TT, m_q2_pe)
        t_q2b = None
        for m2 in range(2):
            t_q2b = rec.op("scalar", lambda e, _m=m2: e.copy(
                q2b[:, _m * QW:(_m + 1) * QW], ps_q[_m]), tick=(m2 == 1))
        m_qbank = t_q2b
        rec.wait("tensor", AT, t_q2b)
        rec.wait("sync", VT, m_store.get("x0_rd", 0))
        t_x1 = None
        for m in range(NCH):
            pb = ps_m[m % 2]
            rec.wait("tensor", m_bank_eng[m % 2] == "scalar" and AT or VT, m_bank[m % 2])
            t_mm = None
            for kc in range(2):
                t_mm = rec.op(
                    "tensor",
                    lambda e, _l=v2_s[:, kc * D + m * 128:kc * D + (m + 1) * 128],
                    _r=q2b[:, kc * QW:(kc + 1) * QW], _p=pb,
                    _s=(kc == 0), _e=(kc == 1):
                    e.matmul(_p, _l, _r, start=_s, stop=_e), tick=(kc == 1))
            d_x0 = rec.dma("sync", x0t, dram["x0T"][m, :, qt * QW:(qt + 1) * QW])
            rec.waits("vector", (TT, t_mm), (DS, d_x0))
            rec.op("vector", lambda e, _o=hhq[:, m * QW:(m + 1) * QW], _p=pb,
                   _b=bvec[:, 7 * NCH + m:7 * NCH + m + 1]:
                   e.scalar_tensor_tensor(_o, _p, _b, _o, ALU.add, ALU.add))
            t_ad = rec.op("vector", lambda e, _o=hhq[:, m * QW:(m + 1) * QW]:
                          e.tensor_add(_o, _o, x0t), tick=True)
            m_bank[m % 2] = t_ad
            m_bank_eng[m % 2] = "vector"
            rec.wait("sync", VT, t_ad)
            t_x1 = rec.op("vector", lambda e, _k=m: e.tensor_copy(
                hhb[:, _k * QW:(_k + 1) * QW], hhq[:, _k * QW:(_k + 1) * QW]),
                tick=(m == NCH - 1))
        m_q2_pe = rec.t("tensor")
        m_store["x0_rd"] = t_x1
        rec.wait("sync", VT, t_x1)
        m_store["x1"] = rec.dma(
            "sync", dram["x1f"][:, :, qt * QW:(qt + 1) * QW].rearrange("c p w -> p c w"),
            hhq[:].rearrange("p (c w) -> p c w", c=NCH))
        m_store["hhq_rd"] = t_x1

        if P3STAGE < 5:
            continue
        x1bc = [hhb[:, k * QW:(k + 1) * QW] for k in range(NCH)]
        marks = ln_feature(ctx, QW, x1bc, sq, rows, ps1, ps2, ps_bc, mu_b, rs_b,
                           [(VT, t_x1)], marks)
        rec.waits("vector", (DS, m_store.get("xn1", 0)), (TT, m_xt_pe))
        t_xn1 = None
        for k in range(NCH):
            rec.op("vector", lambda e, _k=k: e.tensor_sub(
                tt, hhq[:, _k * QW:(_k + 1) * QW], mu_b))
            t_xn1 = rec.op("vector", lambda e, _k=k: e.tensor_mul(
                yh[:, _k * QW:(_k + 1) * QW], tt, rs_b), tick=(k == NCH - 1))
        m_store["hhq_rd"] = t_xn1
        rec.wait("sync", VT, t_xn1)
        m_store["xn1"] = rec.dma(
            "sync", dram["xn1f"][:, :, qt * QW:(qt + 1) * QW].rearrange("c p w -> p c w"),
            yh[:].rearrange("p (c w) -> p c w", c=NCH))
        if P3STAGE < 6:
            continue
        rec.waits("tensor", (VT, t_xn1), (DS, ctx.d_const))
        t_trev = 0
        for tc4 in range(4):
            for kb in range(2):
                if t_trev:
                    rec.wait("tensor", VT, t_trev)
                t_tr = None
                for j in range(4):
                    k = kb * 4 + j
                    t_tr = rec.op(
                        "tensor",
                        lambda e, _i=yh[:, k * QW + tc4 * 128:k * QW + tc4 * 128 + 128],
                        _o=ps_tr[:, j * 128:(j + 1) * 128]:
                        e.transpose(_o, _i, ctx.ident_b), tick=(j == 3))
                rec.wait("vector", TT, t_tr)
                t_trev = rec.op(
                    "vector",
                    lambda e, _o=xtstg[:, tc4 * D + kb * 512:tc4 * D + (kb + 1) * 512],
                    _i=ps_tr[:, 0:512]:
                    e.tensor_copy(_o, _i), tick=True)
        m_xt_pe = rec.t("tensor")
        if P3STAGE < 7:
            continue
        rec.waits("sync", (VT, t_trev), (DS, m_store.get("xn1t", 0)))
        m_store["xn1t"] = rec.dma(
            "sync", dram["xn1t"][qt * 4:(qt + 1) * 4].rearrange("c p w -> p c w"),
            xtstg[:].rearrange("p (c w) -> p c w", c=4))
        m_yh_pe = rec.t("tensor")
        m_pq_pe = rec.t("tensor")
    ph.close()



def phase4a(ctx):
    nc, rec, dram = ctx.nc, ctx.rec, ctx.dram
    rec.wait("gpsimd", rec.dtick["sync"], rec.dtick["sync"].n)
    rec.wait("sync", rec.dtick["sync"], rec.dtick["sync"].n)
    ph = ExitStack()
    S = lambda n, s, dt: ph.enter_context(nc.sbuf_tensor(n, s, dt)).ap()
    P = lambda n: ph.enter_context(nc.psum_tensor(n, [128, QW], F32)).ap()
    TT, VT, AT = rec.tick["tensor"], rec.tick["vector"], rec.tick["scalar"]
    DS, DG = rec.dtick["sync"], rec.dtick["gpsimd"]

    phiT_s = S("p4a_phiT", [128, NCH * 2 * D], BF16)
    phib_f = S("p4a_phibf", [1, 2 * D], F32)
    phib_b = S("p4a_phib", [1, 2 * D], BF16)
    xt_s = S("p4a_xt", [128, NTC * D], BF16)
    xf_s = S("p4a_xf", [128, 2 * NCH * 128], BF16)
    lg = S("p4a_lg", [128, 2 * D], F32)
    ee = S("p4a_e", [128, 2 * D], F32)
    scr = S("p4a_scr", [128, D], F32)
    mx = S("p4a_mx", [128, 4], F32)
    rdD = S("p4a_rdD", [128, 64], F32)
    rdC = S("p4a_rdC", [128, 32], F32)
    dstg = S("p4a_dstg", [128, 2 * D], BF16)
    cstg = S("p4a_cstg", [128, 2 * D], BF16)
    ctr = S("p4a_ctr", [128, 2 * D], BF16)
    dcol = S("p4a_dcol", [128, NTC * 128], BF16)
    xpp = S("p4a_xpp", [128, 2 * QW], F32)

    ps_l = [P("p4a_l0"), P("p4a_l1")]
    ps_x = [P("p4a_x0"), P("p4a_x1")]
    ps_tr = ph.enter_context(nc.psum_tensor("p4a_tr", [128, 1024], BF16)).ap()

    d_phi = rec.dma("gpsimd", phiT_s[:].rearrange("p (c n) -> p c n", c=NCH),
                    dram["phiT"].rearrange("c p n -> p c n"))
    d_pb = rec.dma("sync", phib_f, dram["phib_row"])
    rec.wait("vector", DS, d_pb)
    t_pb = rec.op("vector", lambda e: e.tensor_copy(phib_b, phib_f), tick=True)
    rec.wait("sync", DS, rec.dtick["sync"].n)
    d_xt = rec.dma("sync", xt_s[:].rearrange("p (c n) -> p c n", c=NTC),
                   dram["xn1t"].rearrange("c p n -> p c n"))

    m_lbank = {0: 0, 1: 0}
    m_ee = 0
    m_dstg = 0
    m_tr = 0
    m_xf = {0: 0, 1: 0}
    t_dc = {}
    for tc in range(NTC):
        par = tc % 2
        xfp = xf_s[:, par * NCH * 128:(par + 1) * NCH * 128]
        rec.wait("sync", TT, m_xf[par])
        d_xf = rec.dma("sync", xfp.rearrange("p (c w) -> p c w", c=NCH),
                       dram["xn1f"][:, :, tc * 128:(tc + 1) * 128].rearrange("c p w -> p c w"))
        for nch in range(4):
            pb = ps_l[nch % 2]
            rec.wait("tensor", AT, m_lbank[nch % 2])
            if nch == 0:
                rec.waits("tensor", (DS, d_xf), (DG, d_phi), (VT, t_pb))
            t_mm = None
            for k in range(NCH):
                t_mm = rec.op(
                    "tensor",
                    lambda e, _l=xfp[:, k * 128:(k + 1) * 128],
                    _r=phiT_s[:, k * 2 * D + nch * QW:k * 2 * D + (nch + 1) * QW], _p=pb,
                    _s=(k == 0):
                    e.matmul(_p, _l, _r, start=_s, stop=False, skip_group_check=True),
                    tick=False)
            t_mm = rec.op("tensor", lambda e, _p=pb,
                          _r=phib_b[0:1, nch * QW:(nch + 1) * QW]:
                          e.matmul(_p, ctx.ones_b[0:1, :], _r, start=False, stop=True,
                                   skip_group_check=True), tick=True)
            rec.wait("scalar", TT, t_mm)
            if nch == 0:
                rec.wait("scalar", VT, m_ee)
            m_lbank[nch % 2] = rec.op(
                "scalar", lambda e, _o=lg[:, nch * QW:(nch + 1) * QW], _p=pb:
                e.copy(_o, _p), tick=True)
        t_lg = m_lbank[1]
        rec.wait("vector", AT, t_lg)
        rec.op("vector", lambda e: e.reduce_max(mx[:, 0:1], lg, axis=mybir.AxisListType.X))
        t_mx = rec.op("vector", lambda e: e.tensor_scalar(
            mx[:, 1:2], mx[:, 0:1], -1.0, None, ALU.mult), tick=True)
        rec.wait("scalar", VT, t_mx)
        t_e = rec.op("scalar", lambda e: e.activation(ee, lg, AF.Exp, bias=mx[:, 1:2]),
                     tick=True)
        rec.wait("vector", AT, t_e)
        rec.op("vector", lambda e: e.tensor_reduce(
            rdD, ee.rearrange("p (n q) -> p n q", n=64), mybir.AxisListType.X, ALU.add))
        rec.op("vector", lambda e: e.tensor_add(scr[:, 0:D], ee[:, 0:D], ee[:, D:2 * D]))
        rec.op("vector", lambda e: e.tensor_add(scr[:, 0:QW], scr[:, 0:QW], scr[:, QW:D]))
        rec.op("vector", lambda e: e.tensor_add(scr[:, 0:256], scr[:, 0:256], scr[:, 256:512]))
        rec.op("vector", lambda e: e.tensor_add(scr[:, 0:128], scr[:, 0:128], scr[:, 128:256]))
        rec.op("vector", lambda e: e.tensor_add(scr[:, 0:64], scr[:, 0:64], scr[:, 64:128]))
        rec.op("vector", lambda e: e.tensor_add(rdC, scr[:, 0:32], scr[:, 32:64]))
        rec.op("vector", lambda e: e.reciprocal(rdD, rdD))
        rec.op("vector", lambda e: e.reciprocal(rdC, rdC))
        rec.wait("vector", DS, t_dc.get(tc - 2, 0))
        rec.op("vector", lambda e: e.tensor_tensor(
            dstg.rearrange("p (n q) -> p n q", n=64),
            ee.rearrange("p (n q) -> p n q", n=64),
            rdD.unsqueeze(2).broadcast_to((128, 64, 32)), ALU.mult))
        m_ee = rec.op("vector", lambda e: e.tensor_tensor(
            cstg.rearrange("p (n q) -> p n q", n=64),
            ee.rearrange("p (n q) -> p n q", n=64),
            rdC.unsqueeze(1).broadcast_to((128, 64, 32)), ALU.mult), tick=True)
        rec.wait("sync", VT, m_ee)
        t_dc[tc] = rec.dma("sync", dram["dtok"][tc], dstg)
        rec.wait("tensor", VT, m_ee)
        t_trev = m_tr
        for kb in range(4):
            if t_trev:
                rec.wait("tensor", VT, t_trev)
            t_tr = None
            for j in range(4):
                npc = kb * 4 + j
                t_tr = rec.op(
                    "tensor",
                    lambda e, _i=cstg[:, npc * 128:(npc + 1) * 128],
                    _o=ps_tr[:, j * 128:(j + 1) * 128]:
                    e.transpose(_o, _i, ctx.ident_b), tick=(j == 3))
            rec.wait("vector", TT, t_tr)
            t_trev = rec.op(
                "vector", lambda e, _o=ctr[:, kb * QW:(kb + 1) * QW], _i=ps_tr[:, 0:512]:
                e.tensor_copy(_o, _i), tick=True)
        m_tr = t_trev
        rec.waits("sync", (VT, t_trev))
        t_dc[tc] = rec.dma(
            "sync", dram["cfm"][:, :, tc * 128:(tc + 1) * 128].rearrange("c p w -> p c w"),
            ctr[:].rearrange("p (c w) -> p c w", c=16))
        m_xf[par] = rec.t("tensor")

    d_last = DS.n
    m_xb = {0: 0, 1: 0}
    t_dcol_pe = 0
    for npc in range(16):
        rec.wait("sync", DS, d_last)
        rec.wait("sync", TT, t_dcol_pe)
        d_dc = rec.dma("sync", dcol.rearrange("p (c w) -> p c w", c=NTC),
                       dram["dtok"][:, :, npc * 128:(npc + 1) * 128].rearrange("c p w -> p c w"))
        for dc in range(2):
            pb = ps_x[dc]
            rec.wait("tensor", AT, m_xb[dc])
            if dc == 0:
                rec.waits("tensor", (DS, d_dc), (DS, d_xt))
            t_mm = None
            for tc in range(NTC):
                t_mm = rec.op(
                    "tensor",
                    lambda e, _l=dcol[:, tc * 128:(tc + 1) * 128],
                    _r=xt_s[:, tc * D + dc * QW:tc * D + (dc + 1) * QW], _p=pb,
                    _s=(tc == 0), _e=(tc == NTC - 1):
                    e.matmul(_p, _l, _r, start=_s, stop=_e, skip_group_check=True),
                    tick=(tc == NTC - 1))
            rec.wait("scalar", TT, t_mm)
            rec.wait("scalar", DS, d_last)
            m_xb[dc] = rec.op("scalar", lambda e, _o=xpp[:, dc * QW:(dc + 1) * QW], _p=pb:
                              e.copy(_o, _p), tick=True)
        t_dcol_pe = rec.t("tensor")
        rec.wait("sync", AT, max(m_xb[0], m_xb[1]))
        d_last = rec.dma("sync", dram["xpart_i"][npc], xpp)
    rec.wait("gpsimd", DS, DS.n)
    cc = rec.sem("cc_ar")
    rec.op("gpsimd", lambda e: e.collective_compute(
        "AllReduce", ALU.add, [[0, 1, 2, 3, 4, 5, 6, 7]],
        [dram["xpart_i"].rearrange("a p n -> (a p) n").opt()],
        [dram["xred_i"].rearrange("a p n -> (a p) n").opt()]), tick=False)
    ctx.cc_ar = cc
    rec.q["gpsimd"][-1] = (lambda f: lambda e: f(e).then_inc(cc.h, 1))(rec.q["gpsimd"][-1])
    cc.n = 1
    if ctx.test_mode:
        rec.wait("sync", DS, DS.n)
        rec.dma("sync", dram["xpart"], dram["xpart_i"])
        rec.wait("sync", cc, 1)
        rec.dma("sync", dram["xred"], dram["xred_i"])
    ph.close()



def phase4b(ctx):
    nc, rec, dram = ctx.nc, ctx.rec, ctx.dram
    rec.wait("gpsimd", rec.dtick["sync"], rec.dtick["sync"].n)
    rec.wait("sync", rec.dtick["sync"], rec.dtick["sync"].n)
    ph = ExitStack()
    S = lambda n, s, dt: ph.enter_context(nc.sbuf_tensor(n, s, dt)).ap()
    P = lambda n: ph.enter_context(nc.psum_tensor(n, [128, QW], F32)).ap()
    TT, VT, AT = rec.tick["tensor"], rec.tick["vector"], rec.tick["scalar"]
    DS, DG = rec.dtick["sync"], rec.dtick["gpsimd"]

    wq_s = S("p4b_wq", [128, NCH * D], BF16)
    wk_s = S("p4b_wk", [128, NCH * D], BF16)
    wv_s = S("p4b_wv", [128, NCH * D], BF16)
    xf_s = S("p4b_xf", [128, 2 * NCH * QW], BF16)
    ekf = S("p4b_ekf", [128, T], F32)
    ekfb = S("p4b_ekfb", [128, T], BF16)
    cc_t = S("p4b_c", [128, T], F32)
    qf = S("p4b_qf", [128, T], F32)
    qhat = S("p4b_qhat", [128, T], BF16)
    ektok = S("p4b_ektok", [128, NTC * 128], BF16)
    vtok = S("p4b_vtok", [128, NTC * 128], BF16)
    Tb = S("p4b_Tb", [128, 2 * 128], BF16)
    ATm = S("p4b_ATm", [128, 2 * 64], BF16)
    ostg = S("p4b_ostg", [128, NTC * 128], BF16)
    onesf = S("p4b_onesf", [128, 1], F32)

    ps_p = [P("p4b_p0"), P("p4b_p1")]
    ps_T = P("p4b_T")
    ps_o = [P("p4b_o0"), P("p4b_o1")]
    ps_A = [P("p4b_A0"), P("p4b_A1")]

    d_wq = rec.dma("gpsimd", wq_s[:].rearrange("p (c n) -> p c n", c=NCH),
                   dram["wqT1"].rearrange("c p n -> p c n"))
    d_wk = rec.dma("gpsimd", wk_s[:].rearrange("p (c n) -> p c n", c=NCH),
                   dram["wkT1"].rearrange("c p n -> p c n"))
    d_wv = rec.dma("gpsimd", wv_s[:].rearrange("p (c n) -> p c n", c=NCH),
                   dram["wvT1"].rearrange("c p n -> p c n"))
    rec.op("vector", lambda e: e.memset(onesf, 1.0), tick=True)
    t_ones = rec.t("vector")

    m_stg = {}
    m_proj = {0: 0, 1: 0}
    m_xf = {0: 0, 1: 0}
    m_ekf_rd = 0
    m_cs_rd = 0
    m_chunk = {}
    for h in range(8):
        t_kev = t_qev = t_vev = t_eev = None
        for qt in range(NQT):
            par = qt % 2
            xfp = xf_s[:, par * NCH * QW:(par + 1) * NCH * QW]
            if h > 0 or qt >= 2:
                rec.wait("sync", TT, m_xf[par])
            d_xf = rec.dma("sync", xfp.rearrange("p (c w) -> p c w", c=NCH),
                           dram["xn1f"][:, :, qt * QW:(qt + 1) * QW]
                           .rearrange("c p w -> p c w"))
            for pi, (wsl, dst, func) in enumerate(
                    [(wk_s, ekf, AF.Exp), (wq_s, qf, AF.Silu)]):
                pb = ps_p[pi]
                rec.wait("tensor", AT, m_proj[pi])
                if qt == 0 and pi == 0:
                    rec.waits("tensor", (DG, d_wv), (VT, m_ekf_rd))
                rec.wait("tensor", DS, d_xf)
                t_mm = None
                for k in range(NCH):
                    t_mm = rec.op(
                        "tensor",
                        lambda e, _l=wsl[:, k * D + h * 128:k * D + (h + 1) * 128],
                        _r=xfp[:, k * QW:(k + 1) * QW], _p=pb,
                        _s=(k == 0), _e=(k == NCH - 1):
                        e.matmul(_p, _l, _r, start=_s, stop=_e, skip_group_check=True),
                        tick=(k == NCH - 1))
                rec.wait("scalar", TT, t_mm)
                m_proj[pi] = rec.op(
                    "scalar", lambda e, _o=dst[:, qt * QW:(qt + 1) * QW], _p=pb,
                    _f=func: e.activation(_o, _p, _f), tick=True)
                if pi == 0:
                    t_kev = m_proj[pi]
                else:
                    t_qev = m_proj[pi]
            for tc4 in range(4):
                tc = qt * 4 + tc4
                for pi2, (wsl, dst, func) in enumerate(
                        [(wv_s, vtok, AF.Identity), (wk_s, ektok, AF.Exp)]):
                    pb = ps_p[pi2]
                    rec.wait("tensor", AT, m_proj[pi2])
                    t_mm = None
                    for k in range(NCH):
                        t_mm = rec.op(
                            "tensor",
                            lambda e, _l=xfp[:, k * QW + tc4 * 128:k * QW + tc4 * 128 + 128],
                            _r=wsl[:, k * D + h * 128:k * D + (h + 1) * 128], _p=pb,
                            _s=(k == 0), _e=(k == NCH - 1):
                            e.matmul(_p[0:128, 0:128], _l, _r, start=_s, stop=_e,
                                     skip_group_check=True), tick=(k == NCH - 1))
                    rec.wait("scalar", TT, t_mm)
                    m_proj[pi2] = rec.op(
                        "scalar", lambda e, _o=dst[:, tc * 128:(tc + 1) * 128],
                        _p=pb, _f=func:
                        e.activation(_o, _p[0:128, 0:128], _f), tick=True)
                    if pi2 == 0:
                        t_vev = m_proj[pi2]
                    else:
                        t_eev = m_proj[pi2]
            m_xf[par] = rec.t("tensor")
        rec.waits("vector", (AT, t_kev), (VT, t_ones), (TT, m_cs_rd))
        t_c = rec.op("vector", lambda e: e.tensor_tensor_scan(
            cc_t, ekf, onesf.broadcast_to((128, T)), 0.0, ALU.add, ALU.mult), tick=True)
        t_ekb = rec.op("vector", lambda e: e.tensor_copy(ekfb, ekf), tick=True)
        rec.wait("scalar", VT, t_c)
        rec.op("scalar", lambda e: e.activation(cc_t, cc_t, AF.Ln))
        t_rc = rec.op("scalar", lambda e: e.activation(cc_t, cc_t, AF.Exp, scale=-1.0),
                      tick=True)
        rec.waits("vector", (AT, max(t_rc, t_qev)), (TT, m_cs_rd))
        t_qh = rec.op("vector", lambda e: e.tensor_mul(qhat.bitcast(F32) if False else qhat,
                                                       qf, cc_t), tick=True)
        t_snap = {}
        t_oev = {}
        t_aev = {}
        t_U = {}
        for n in range(64):
            pn = n % 2
            base = (n % 2) * 64
            tcn = n // 2
            rec.wait("tensor", VT, max(t_qh, t_ekb))
            if n >= 2:
                rec.wait("tensor", VT, t_aev[n - 2])
            t_A = rec.op(
                "tensor",
                lambda e, _l=ekfb[:, n * 64:(n + 1) * 64],
                _r=qhat[:, n * 64:(n + 1) * 64], _p=ps_A[pn]:
                e.matmul(_p[0:64, 0:64], _l, _r, start=True, stop=True,
                         skip_group_check=True), tick=True)
            rec.wait("vector", TT, t_A)
            t_aev[n] = rec.op(
                "vector",
                lambda e, _o=ATm[base:base + 64, (n % 2) * 64:(n % 2) * 64 + 64],
                _p=ps_A[pn]:
                e.tensor_mul(_o, _p[0:64, 0:64], ctx.triu_f), tick=True)
            if n > 0:
                rec.wait("scalar", TT, t_U[n - 1])
            rec.wait("scalar", AT, m_chunk.get(("snapfree", pn), 0))
            if n == 0:
                rec.wait("scalar", TT, m_chunk.get("Tclear", 0))
            t_snap[n] = rec.op(
                "scalar", lambda e, _o=Tb[:, pn * 128:(pn + 1) * 128], _n=n:
                (e.copy(_o, ps_T[:, 0:128]) if _n > 0 else e.memzero(_o)), tick=True)
            m_chunk[("snapfree", pn)] = t_snap[n]
            rec.waits("tensor", (AT, t_snap[n]), (VT, t_aev[n]))
            if n >= 2:
                rec.wait("tensor", AT, t_oev[n - 2])
            rec.op("tensor",
                   lambda e, _l=qhat[:, n * 64:(n + 1) * 64],
                   _r=Tb[:, pn * 128:(pn + 1) * 128], _p=ps_o[pn]:
                   e.matmul(_p[0:64, 0:128], _l, _r, start=True, stop=False,
                            skip_group_check=True))
            t_o = rec.op(
                "tensor",
                lambda e, _l=ATm[base:base + 64, (n % 2) * 64:(n % 2) * 64 + 64],
                _r=vtok[base:base + 64, tcn * 128:(tcn + 1) * 128], _p=ps_o[pn]:
                e.matmul(_p[0:64, 0:128], _l, _r, start=False, stop=True,
                         skip_group_check=True), tick=True)
            t_U[n] = rec.op(
                "tensor",
                lambda e, _l=ektok[base:base + 64, tcn * 128:(tcn + 1) * 128],
                _r=vtok[base:base + 64, tcn * 128:(tcn + 1) * 128], _p=ps_T,
                _s=(n == 0):
                e.matmul(_p[0:128, 0:128], _l, _r, start=_s, stop=(n == 63),
                         skip_group_check=True), tick=True)
            rec.wait("scalar", TT, t_o)
            if h > 0:
                rec.wait("scalar", DS, m_stg.get("o", 0))
            t_oev[n] = rec.op(
                "scalar",
                lambda e, _o=ostg[base:base + 64, tcn * 128:(tcn + 1) * 128],
                _p=ps_o[pn]:
                e.copy(_o, _p[0:64, 0:128]), tick=True)
        m_chunk["Tclear"] = t_U[63]
        m_cs_rd = rec.t("tensor")
        m_ekf_rd = rec.t("vector")
        rec.wait("sync", AT, t_oev[63])
        m_stg["o"] = rec.dma(
            "sync", dram["otok"][:, :, h * 128:(h + 1) * 128].rearrange("c p w -> p c w"),
            ostg[:].rearrange("p (c w) -> p c w", c=NTC))
    ph.close()



def phase4c(ctx):
    nc, rec, dram = ctx.nc, ctx.rec, ctx.dram
    rec.wait("gpsimd", rec.dtick["sync"], rec.dtick["sync"].n)
    rec.wait("sync", rec.dtick["sync"], rec.dtick["sync"].n)
    ph = ExitStack()
    S = lambda n, s, dt: ph.enter_context(nc.sbuf_tensor(n, s, dt)).ap()
    P = lambda n: ph.enter_context(nc.psum_tensor(n, [128, QW], F32)).ap()
    TT, VT, AT = rec.tick["tensor"], rec.tick["vector"], rec.tick["scalar"]
    DS, DG = rec.dtick["sync"], rec.dtick["gpsimd"]

    wg1_s = S("p4c_wg1", [128, NCH * 128], BF16)
    wg2_s = S("p4c_wg2", [128, D], BF16)
    g1_s = S("p4c_g1", [128, T], BF16)
    gnw_f = S("p4c_gnwf", [1, D], F32)
    gnw_b = S("p4c_gnw", [128, D], F32)
    xf_s = S("p4c_xf", [128, 2 * NCH * QW], BF16)
    otile = S("p4c_ot", [128, 2 * D], BF16)
    sg = S("p4c_sg", [128, D], BF16)
    stat = S("p4c_stat", [128, 4], F32)
    op_t = S("p4c_opt", [128, D], BF16)
    otr = S("p4c_otr", [128, NCH * 128], BF16)

    ps_g = [P("p4c_g0"), P("p4c_g1b")]
    ps_tr = ph.enter_context(nc.psum_tensor("p4c_tr", [128, 1024], BF16)).ap()

    d_wg1 = rec.dma("gpsimd", wg1_s[:].rearrange("p (c n) -> p c n", c=NCH),
                    dram["wg1T"].rearrange("c p n -> p c n"))
    d_wg2 = rec.dma("gpsimd", wg2_s, dram["wg2T"][0])
    d_gnw = rec.dma("sync", gnw_f, dram["gnw_row"])
    rec.waits("tensor", (DS, d_gnw), (VT, ctx.t_const))
    t_gb = None
    for n2 in range(2):
        t_gb = rec.op("tensor", lambda e, _n=n2: e.matmul(
            ps_g[_n][:, 0:QW], ctx.ones_f[0:1, :], gnw_f[0:1, _n * QW:(_n + 1) * QW],
            start=True, stop=True), tick=(n2 == 1))
    rec.wait("vector", TT, t_gb)
    t_gnw = rec.op("vector", lambda e: e.tensor_copy(gnw_b[:, 0:QW], ps_g[0][:, 0:QW]))
    t_gnw = rec.op("vector", lambda e: e.tensor_copy(gnw_b[:, QW:D], ps_g[1][:, 0:QW]),
                   tick=True)

    m_xf = {0: 0, 1: 0}
    m_gb = {0: t_gnw, 1: t_gnw}
    for qt in range(NQT):
        par = qt % 2
        xfp = xf_s[:, par * NCH * QW:(par + 1) * NCH * QW]
        if qt >= 2:
            rec.wait("sync", TT, m_xf[par])
        d_xf = rec.dma("sync", xfp.rearrange("p (c w) -> p c w", c=NCH),
                       dram["xn1f"][:, :, qt * QW:(qt + 1) * QW].rearrange("c p w -> p c w"))
        pb = ps_g[qt % 2]
        rec.wait("tensor", VT if qt < 2 else AT, m_gb[qt % 2])
        rec.waits("tensor", (DS, d_xf), (DG, d_wg1))
        t_mm = None
        for k in range(NCH):
            t_mm = rec.op(
                "tensor",
                lambda e, _l=wg1_s[:, k * 128:(k + 1) * 128],
                _r=xfp[:, k * QW:(k + 1) * QW], _p=pb,
                _s=(k == 0), _e=(k == NCH - 1):
                e.matmul(_p, _l, _r, start=_s, stop=_e, skip_group_check=True),
                tick=(k == NCH - 1))
        rec.wait("scalar", TT, t_mm)
        m_gb[qt % 2] = rec.op(
            "scalar", lambda e, _o=g1_s[:, qt * QW:(qt + 1) * QW], _p=pb:
            e.copy(_o, _p), tick=True)
        m_xf[par] = rec.t("tensor")
    t_g1 = m_gb[(NQT - 1) % 2]

    m_ot = 0
    m_tr = 0
    m_opst = {}
    for tc in range(NTC):
        rec.wait("sync", VT, m_ot)
        d_o = rec.dma("sync", otile[:, 0:D], dram["otok"][tc])
        rec.waits("tensor", (AT, t_g1), (DG, d_wg2))
        rec.wait("tensor", AT, m_gb[0])
        rec.wait("tensor", AT, m_gb[1])
        t_mm = None
        for n2 in range(2):
            t_mm = rec.op(
                "tensor",
                lambda e, _l=g1_s[:, tc * 128:(tc + 1) * 128],
                _r=wg2_s[:, n2 * QW:(n2 + 1) * QW], _p=ps_g[n2]:
                e.matmul(_p, _l, _r, start=True, stop=True,
                         skip_group_check=True), tick=(n2 == 1))
        rec.wait("scalar", TT, t_mm)
        rec.wait("scalar", VT, m_ot)
        t_sg = None
        for n2 in range(2):
            t_sg = rec.op("scalar", lambda e, _o=sg[:, n2 * QW:(n2 + 1) * QW],
                          _p=ps_g[n2]: e.activation(_o, _p, AF.Silu), tick=(n2 == 1))
        m_gb[0] = t_sg
        m_gb[1] = t_sg
        rec.wait("scalar", DS, d_o)
        rec.op("scalar", lambda e: e.activation(
            otile[:, D:2 * D], otile[:, 0:D], AF.Square, accum_out=stat[:, 0:1]),
            tick=True)
        rec.op("scalar", lambda e: e.activation(stat[:, 1:2], stat[:, 0:1], AF.Ln,
                                                bias=ctx.cpool[:, 0:1], scale=1.0 / D))
        t_rs = rec.op("scalar", lambda e: e.activation(stat[:, 1:2], stat[:, 1:2],
                                                       AF.Exp, scale=-0.5), tick=True)
        rec.wait("vector", AT, max(t_rs, t_sg))
        rec.op("vector", lambda e: e.tensor_scalar(
            op_t, otile[:, 0:D], stat[:, 1:2], None, ALU.mult))
        rec.op("vector", lambda e: e.tensor_mul(op_t, op_t, gnw_b))
        m_ot = rec.op("vector", lambda e: e.tensor_mul(op_t, op_t, sg), tick=True)
        t_trev = m_tr
        rec.wait("tensor", VT, m_ot)
        for kb in range(2):
            if t_trev:
                rec.wait("tensor", VT, t_trev)
            t_tr = None
            for j in range(4):
                k = kb * 4 + j
                t_tr = rec.op(
                    "tensor",
                    lambda e, _i=op_t[:, k * 128:(k + 1) * 128],
                    _o=ps_tr[:, j * 128:(j + 1) * 128]:
                    e.transpose(_o, _i, ctx.ident_b), tick=(j == 3))
            rec.wait("vector", TT, t_tr)
            t_trev = rec.op(
                "vector", lambda e, _o=otr[:, kb * QW:(kb + 1) * QW], _i=ps_tr[:, 0:512]:
                e.tensor_copy(_o, _i), tick=True)
        m_tr = t_trev
        rec.waits("sync", (VT, t_trev), (DS, m_opst.get(tc - 2, 0)))
        m_opst[tc] = rec.dma(
            "sync", dram["opfm"][:, :, tc * 128:(tc + 1) * 128].rearrange("c p w -> p c w"),
            otr[:].rearrange("p (c w) -> p c w", c=NCH))
    ph.close()



def phase4d(ctx):
    nc, rec, dram = ctx.nc, ctx.rec, ctx.dram
    rec.wait("gpsimd", rec.dtick["sync"], rec.dtick["sync"].n)
    rec.wait("sync", rec.dtick["sync"], rec.dtick["sync"].n)
    if getattr(ctx, "cc_ar", None) is not None:
        rec.wait("sync", ctx.cc_ar, 1)
        rec.wait("gpsimd", ctx.cc_ar, 1)
        rec.wait("tensor", ctx.cc_ar, 1)
    ph = ExitStack()
    S = lambda n, s, dt: ph.enter_context(nc.sbuf_tensor(n, s, dt)).ap()
    P = lambda n: ph.enter_context(nc.psum_tensor(n, [128, QW], F32)).ap()
    TT, VT, AT = rec.tick["tensor"], rec.tick["vector"], rec.tick["scalar"]
    DS, DG = rec.dtick["sync"], rec.dtick["gpsimd"]

    xr = S("p4d_xr", [128, D], F32)
    xrt = S("p4d_xrt", [128, 2 * D], F32)
    selx_s = S("p4d_selx", [128, 16 * 256], F32)
    xsl = S("p4d_xsl", [128, 2 * D], F32)
    sq4 = S("p4d_sq", [128, D], BF16)
    xh_t = S("p4d_xht", [128, D], BF16)
    xh_f = S("p4d_xhf", [128, NCH * 256], BF16)
    stat = S("p4d_stat", [128, 4], F32)
    ew_s = S("p4d_ew", [128, 8 * NCH * D], BF16)
    eb_s = S("p4d_eb", [4, 2 * D], BF16)
    eb_f = S("p4d_ebf", [4, 2 * D], F32)
    sel_s = S("p4d_sel", [4, 128], BF16)
    sel_f = S("p4d_self", [4, 128], F32)
    yst = S("p4d_yst", [128, D], F32)

    ps_y = [P("p4d_y0"), P("p4d_y1")]
    ps_z = [P("p4d_z0"), P("p4d_z1")]
    ps_tr = ph.enter_context(nc.psum_tensor("p4d_tr", [128, 1024], BF16)).ap()

    if ctx.test_mode and getattr(ctx, "cc_ar", None) is None:
        d_cp0 = rec.dma("sync", dram["xred_i"], dram["xred"])
        rec.wait("sync", rec.dtick["sync"], d_cp0)
    d_sel = rec.dma("sync", sel_f, dram["sel4c"])
    d_eb = rec.dma("sync", eb_f.rearrange("p (a n) -> p a n", a=2),
                   dram["eb4"].rearrange("a p n -> p a n"))
    d_selx = rec.dma("sync", selx_s[:].rearrange("p (c n) -> p c n", c=16),
                     dram["selx"].rearrange("c p n -> p c n"))
    rec.wait("vector", DS, d_eb)
    rec.op("vector", lambda e: e.tensor_copy(sel_s, sel_f))
    t_sel = rec.op("vector", lambda e: e.tensor_copy(eb_s, eb_f), tick=True)
    d_ew = {}
    m_ewrd = {0: 0, 1: 0}

    t_selmm = {}
    m_xrt = {0: 0, 1: 0}
    for npc in range(16):
        par = npc % 2
        rec.wait("sync", TT, m_xrt[par])
        d_xrt = rec.dma("sync", xrt[:, par * D:(par + 1) * D], dram["xred_i"][npc])
        rec.wait("tensor", DS, d_xrt)
        if npc == 0:
            rec.wait("tensor", DS, d_selx)
        for m2 in range(2):
            for n2 in range(2):
                t_selmm[(m2, n2)] = rec.op(
                    "tensor",
                    lambda e, _l=selx_s[:, npc * 256 + m2 * 128:npc * 256 + (m2 + 1) * 128],
                    _r=xrt[:, par * D + n2 * QW:par * D + (n2 + 1) * QW],
                    _p=(ps_z if m2 else ps_y)[n2],
                    _s=(npc == 0), _e=(npc == 15):
                    e.matmul(_p, _l, _r, start=_s, stop=_e, skip_group_check=True),
                    tick=(m2 == 1 and n2 == 1))
        m_xrt[par] = rec.t("tensor")
    rec.wait("scalar", TT, t_selmm[(1, 1)])
    t_xsl = {}
    for m2 in range(2):
        for n2 in range(2):
            t_xsl[m2] = rec.op(
                "scalar", lambda e, _o=xsl[:, m2 * D + n2 * QW:m2 * D + (n2 + 1) * QW],
                _p=(ps_z if m2 else ps_y)[n2]:
                e.copy(_o, _p), tick=(n2 == 1))

    if "xsldbg" in dram:
        rec.wait("sync", AT, t_xsl[1])
        rec.dma("sync", dram["xsldbg"], xsl)
    m_xr = 0
    m_tr = 0
    t_xh = 0
    for npc in range(2):
        rec.wait("vector", AT, t_xsl[npc])
        rec.wait("vector", AT, m_xr)
        rec.op("vector", lambda e, _n=npc: e.tensor_copy(xr, xsl[:, _n * D:(_n + 1) * D]))
        rec.op("vector", lambda e: e.tensor_scalar(
            xr, xr, -33000.0, 65000.0, ALU.max, ALU.min))
        t_mu = rec.op("vector", lambda e: e.reduce_sum(
            stat[:, 0:1], xr, axis=mybir.AxisListType.X), tick=True)
        rec.wait("scalar", VT, t_mu)
        rec.op("scalar", lambda e: e.activation(sq4, xr, AF.Square,
                                                accum_out=stat[:, 1:2]))
        t_s2 = rec.op("scalar", lambda e: e.mul(stat[:, 1:2], stat[:, 1:2], 1.0),
                      tick=True)
        rec.wait("vector", AT, t_s2)
        rec.op("vector", lambda e: e.tensor_scalar(stat[:, 0:1], stat[:, 0:1],
                                                   1.0 / D, None, ALU.mult))
        rec.op("vector", lambda e: e.tensor_scalar(stat[:, 1:2], stat[:, 1:2],
                                                   1.0 / D, None, ALU.mult))
        rec.op("vector", lambda e: e.tensor_mul(stat[:, 2:3], stat[:, 0:1], stat[:, 0:1]))
        t_var = rec.op("vector", lambda e: e.tensor_sub(stat[:, 1:2], stat[:, 1:2],
                                                        stat[:, 2:3]), tick=True)
        rec.wait("scalar", VT, t_var)
        rec.op("scalar", lambda e: e.activation(stat[:, 1:2], stat[:, 1:2], AF.Ln,
                                                bias=ctx.cpool[:, 0:1]))
        t_rst = rec.op("scalar", lambda e: e.activation(stat[:, 1:2], stat[:, 1:2],
                                                        AF.Exp, scale=-0.5), tick=True)
        rec.wait("vector", AT, t_rst)
        rec.op("vector", lambda e: e.tensor_mul(stat[:, 2:3], stat[:, 0:1], stat[:, 1:2]))
        t_nb = rec.op("vector", lambda e: e.tensor_scalar(
            stat[:, 2:3], stat[:, 2:3], -1.0, None, ALU.mult), tick=True)
        rec.wait("scalar", VT, t_nb)
        t_tanh = rec.op("scalar", lambda e: e.activation(
            xh_t, xr, AF.Tanh, bias=stat[:, 2:3], scale=stat[:, 1:2]), tick=True)
        m_xr = t_tanh
        if "xhdbg" in dram:
            rec.wait("sync", AT, t_tanh)
            d_dbg = rec.dma("sync", dram["xhdbg"][npc], xh_t)
            rec.wait("scalar", DS, d_dbg)
        rec.wait("tensor", AT, t_tanh)
        t_trev = m_tr
        for kb in range(2):
            if t_trev:
                rec.wait("tensor", VT, t_trev)
            t_tr = None
            for j in range(4):
                dc = kb * 4 + j
                t_tr = rec.op(
                    "tensor",
                    lambda e, _i=xh_t[:, dc * 128:(dc + 1) * 128],
                    _o=ps_tr[:, j * 128:(j + 1) * 128]:
                    e.transpose(_o, _i, ctx.ident_b), tick=(j == 3))
            rec.wait("vector", TT, t_tr)
            t_trev = rec.op(
                "vector",
                lambda e, _i=ps_tr[:, 0:512].rearrange("p (c n) -> p c n", c=4),
                _o=xh_f[:].rearrange("p (c n) -> p c n", c=NCH)
                [:, kb * 4:(kb + 1) * 4, npc * 128:(npc + 1) * 128]:
                e.tensor_copy(_o, _i), tick=True)
        m_tr = t_trev
        t_xh = t_trev

    for e8 in range(8):
        d_ew[e8] = rec.dma("gpsimd",
                           ew_s[:, e8 * NCH * D:(e8 + 1) * NCH * D]
                           .rearrange("p (c n) -> p c n", c=NCH),
                           dram["ew"][e8].rearrange("c p n -> p c n"))
    m_yst = max(t_xsl[0], t_xsl[1])
    d_yst = 0
    for grp in range(2):
        for n2 in range(2):
            pb = ps_y[n2]
            rec.waits("tensor", (DS, d_eb), (VT, max(t_sel, t_xh)))
            rec.wait("tensor", AT, m_yst)
            rec.op("tensor", lambda e, _p=pb, _g=grp, _n=n2: e.matmul(
                _p, sel_s, eb_s[:, _g * D + _n * QW:_g * D + (_n + 1) * QW],
                start=True, stop=False, skip_group_check=True))
            t_mm = None
            for le in range(4):
                e8 = grp * 4 + le
                rec.wait("tensor", DG, d_ew[e8])
                for dc in range(NCH):
                    t_mm = rec.op(
                        "tensor",
                        lambda e, _p=pb, _le=le,
                        _l=xh_f[:, dc * 256 + (grp * 4 + le) * 32:
                                dc * 256 + (grp * 4 + le) * 32 + 32],
                        _r=ew_s[:, (grp * 4 + le) * NCH * D + dc * D + n2 * QW:
                                (grp * 4 + le) * NCH * D + dc * D + (n2 + 1) * QW],
                        _e=(dc == NCH - 1):
                        e.matmul(_p[_le * 32:(_le + 1) * 32, :], _l, _r,
                                 start=False, stop=_e, tile_position=(0, _le * 32),
                                 skip_group_check=True),
                        tick=(dc == NCH - 1 and le == 3))
            rec.wait("scalar", TT, t_mm)
            rec.wait("scalar", DS, d_yst)
            m_yst = rec.op("scalar", lambda e, _p=pb, _n=n2: e.copy(
                yst[:, _n * QW:(_n + 1) * QW], _p), tick=True)
        rec.wait("sync", AT, m_yst)
        d_yst = rec.dma("sync", dram["ypart_i"][grp], yst)
    rec.wait("gpsimd", DS, DS.n)
    cc = rec.sem("cc_ag")
    rec.op("gpsimd", lambda e: e.collective_compute(
        "AllGather", ALU.bypass, [[0, 1, 2, 3, 4, 5, 6, 7]],
        [dram["ypart_i"].rearrange("a p n -> (a p) n").opt()],
        [dram["ytld_i"].rearrange("a p n -> (a p) n").opt()]), tick=False)
    ctx.cc_ag = cc
    rec.q["gpsimd"][-1] = (lambda f: lambda e: f(e).then_inc(cc.h, 1))(rec.q["gpsimd"][-1])
    cc.n = 1
    if ctx.test_mode:
        rec.wait("sync", DS, DS.n)
        rec.dma("sync", dram["ypart"], dram["ypart_i"])
        rec.wait("sync", cc, 1)
        rec.dma("sync", dram["ytld"], dram["ytld_i"])
    ph.close()



def phase4e(ctx):
    nc, rec, dram = ctx.nc, ctx.rec, ctx.dram
    rec.wait("gpsimd", rec.dtick["sync"], rec.dtick["sync"].n)
    rec.wait("sync", rec.dtick["sync"], rec.dtick["sync"].n)
    if getattr(ctx, "cc_ag", None) is not None:
        rec.wait("sync", ctx.cc_ag, 1)
        rec.wait("gpsimd", ctx.cc_ag, 1)
    ph = ExitStack()
    S = lambda n, s, dt: ph.enter_context(nc.sbuf_tensor(n, s, dt)).ap()
    P = lambda n: ph.enter_context(nc.psum_tensor(n, [128, QW], F32)).ap()
    TT, VT, AT = rec.tick["tensor"], rec.tick["vector"], rec.tick["scalar"]
    DS, DG = rec.dtick["sync"], rec.dtick["gpsimd"]

    wo_s = S("p4e_wo", [128, NCH * D], BF16)
    yt_s = S("p4e_yt", [128, 16 * D], BF16)
    opq = S("p4e_opq", [128, NCH * QW], BF16)
    cfq = S("p4e_cfq", [128, 16 * QW], BF16)
    x1q = S("p4e_x1q", [128, NCH * QW], F32)
    ostg = S("p4e_ostg", [128, NCH * QW], F32)

    ps_o = [P("p4e_o0"), P("p4e_o1")]

    if ctx.test_mode and getattr(ctx, "cc_ag", None) is None:
        d_cp = rec.dma("sync", dram["ytld_i"], dram["ytld"])
        rec.wait("gpsimd", rec.dtick["sync"], d_cp)
    d_wo = rec.dma("gpsimd", wo_s[:].rearrange("p (c n) -> p c n", c=NCH),
                   dram["woT1"].rearrange("c p n -> p c n"))
    d_yt = rec.dma("gpsimd", yt_s[:].rearrange("p (c n) -> p c n", c=16),
                   dram["ytld_i"].rearrange("c p n -> p c n"))

    m_ld = {}
    m_bank = {0: 0, 1: 0}
    m_st = 0
    for qt in range(NQT):
        rec.wait("sync", TT, m_ld.get("pe", 0))
        d_op = rec.dma("sync", opq.rearrange("p (c w) -> p c w", c=NCH),
                       dram["opfm"][:, :, qt * QW:(qt + 1) * QW].rearrange("c p w -> p c w"))
        d_cf = rec.dma("sync", cfq.rearrange("p (c w) -> p c w", c=16),
                       dram["cfm"][:, :, qt * QW:(qt + 1) * QW].rearrange("c p w -> p c w"))
        rec.wait("sync", VT, m_ld.get("x1", 0))
        d_x1 = rec.dma("sync", x1q.rearrange("p (c w) -> p c w", c=NCH),
                       dram["x1f"][:, :, qt * QW:(qt + 1) * QW].rearrange("c p w -> p c w"))
        t_add = 0
        for m in range(NCH):
            pb = ps_o[m % 2]
            rec.wait("tensor", VT, m_bank[m % 2])
            if m == 0:
                rec.waits("tensor", (DS, d_op), (DS, d_cf), (DG, d_wo), (DG, d_yt))
            t_mm = None
            for k in range(NCH):
                t_mm = rec.op(
                    "tensor",
                    lambda e, _l=wo_s[:, k * D + m * 128:k * D + (m + 1) * 128],
                    _r=opq[:, k * QW:(k + 1) * QW], _p=pb,
                    _s=(k == 0):
                    e.matmul(_p, _l, _r, start=_s, stop=False, skip_group_check=True),
                    tick=False)
            for k in range(16):
                t_mm = rec.op(
                    "tensor",
                    lambda e, _l=yt_s[:, k * D + m * 128:k * D + (m + 1) * 128],
                    _r=cfq[:, k * QW:(k + 1) * QW], _p=pb,
                    _e=(k == 15):
                    e.matmul(_p, _l, _r, start=False, stop=_e, skip_group_check=True),
                    tick=(k == 15))
            rec.waits("vector", (TT, t_mm), (DS, d_x1), (DS, m_st))
            t_add = rec.op("vector", lambda e, _o=ostg[:, m * QW:(m + 1) * QW], _p=pb,
                           _x=x1q[:, m * QW:(m + 1) * QW]:
                           e.tensor_add(_o, _p, _x), tick=True)
            m_bank[m % 2] = t_add
        m_ld["pe"] = rec.t("tensor")
        m_ld["x1"] = t_add
        rec.wait("sync", VT, t_add)
        m_st = rec.dma(
            "sync", dram["outT"][:, :, qt * QW:(qt + 1) * QW].rearrange("c p w -> p c w"),
            ostg[:].rearrange("p (c w) -> p c w", c=NCH))
    ph.close()



PHASE_FNS = {"p1": phase1, "p2": phase2, "p3": phase3, "p4a": phase4a,
             "p4b": phase4b, "p4c": phase4c, "p4d": phase4d, "p4e": phase4e}


def build_program(phases=("p1",), test_mode=False):
    nc = bass.Bass("TRN2", target_bir_lowering=False)
    dram = declare_drams(nc, phases, test_mode)
    es = ExitStack()
    with es:
        rec = Rec(nc, es)
        ctx = Ctx(nc, es, rec, dram, test_mode=test_mode)
        for phn in phases:
            PHASE_FNS[phn](ctx)
        for qe in ("sync", "gpsimd"):
            rec.wait(qe, rec.dtick["sync"], rec.dtick["sync"].n)
            rec.wait(qe, rec.dtick["gpsimd"], rec.dtick["gpsimd"].n)
        with nc.Block() as block:
            rec.emit(block)
    return nc


def _selx(core):
    s = np.zeros((2048, 256), np.float32)
    for j in range(256):
        s[core * 256 + j, j] = 1.0
    return np.ascontiguousarray(s.reshape(16, 128, 256))


def _prep_core_inputs(inputs, core):
    g = lambda k: np.asarray(inputs[k])
    fm = lambda a: np.ascontiguousarray(np.asarray(a, np.float32).reshape(NCH, 128, -1))
    col = lambda v, n=NCH: np.ascontiguousarray(np.asarray(v, np.float32).reshape(n, 128).T)
    row = lambda v: np.ascontiguousarray(np.asarray(v, np.float32).reshape(1, -1))
    x = np.asarray(g("L0_attn_x")[core], np.float32)
    d = {
        "x0T": np.ascontiguousarray(x.T.reshape(NCH, 128, T)),
        "wq": fm(g("L0_attn_Wq")), "wk": fm(g("L0_attn_Wk")), "wv": fm(g("L0_attn_Wv")),
        "lnq_s": col(g("L0_attn_lnq_s")), "lnk_s": col(g("L0_attn_lnk_s")),
        "lnv_s": col(g("L0_attn_lnv_s")),
        "lnq_b": col(g("L0_attn_lnq_b")), "lnk_b": col(g("L0_attn_lnk_b")),
        "lnv_b": col(g("L0_attn_lnv_b")),
        "bq": col(g("L0_attn_bq")), "bk": col(g("L0_attn_bk")),
        "bv_row": row(g("L0_attn_bv")),
        "u1": fm(g("L0_ffn_U1")),
        "wo": fm(g("L0_attn_Wo")), "bo": col(g("L0_attn_bo")),
        "w1": fm(g("L0_attn_W1")), "lnm_s": col(g("L0_attn_lnm_s")),
        "lnm_b": col(g("L0_attn_lnm_b")), "b1": col(g("L0_attn_b1"), 2 * NCH),
        "w2": fm(g("L0_attn_W2")), "b2": col(g("L0_attn_b2")),
        "v1": fm(g("L0_ffn_V1")), "b1f_row": row(g("L0_ffn_b1")),
        "b1f": col(g("L0_ffn_b1"), 4 * NCH),
        "u2": fm(g("L0_ffn_U2")), "v2": fm(g("L0_ffn_V2")), "b2f": col(g("L0_ffn_b2")),
        "phiT": fm(np.asarray(g("L1_ffn_phi_w"), np.float32).T),
        "phib_row": row(g("L1_ffn_phi_b")),
        "wqT1": fm(np.asarray(g("L1_attn_Wq"), np.float32).T),
        "wkT1": fm(np.asarray(g("L1_attn_Wk"), np.float32).T),
        "wvT1": fm(np.asarray(g("L1_attn_Wv"), np.float32).T),
        "wg1T": fm(np.asarray(g("L1_attn_Wg1"), np.float32).T),
        "wg2T": np.ascontiguousarray(np.asarray(g("L1_attn_Wg2"), np.float32).T
                                     .reshape(1, 128, D)),
        "gnw_row": row(g("L1_attn_g_norm_w")),
        "woT1": fm(np.asarray(g("L1_attn_Wo"), np.float32).T),
        "ew": np.ascontiguousarray(
            np.asarray(g("L1_ffn_expert_w"), np.float32)[core * 8:(core + 1) * 8]
            .reshape(8, NCH, 128, D)),
        "eb4": np.ascontiguousarray(
            np.asarray(g("L1_ffn_expert_b"), np.float32)[core * 8:(core + 1) * 8]
            .reshape(2, 4, D)),
        "sel4c": np.repeat(np.eye(4, dtype=np.float32), 32, axis=1),
        "selx": _selx(core),
        "identc": np.eye(128, dtype=np.float32).astype(ml_dtypes.bfloat16),
        "triuc": np.triu(np.ones((64, 64), np.float32)),
        "maskc": np.concatenate([np.zeros((63, 1), np.float32),
                                 np.full((1, 1), -31968.0, np.float32)]),
    }
    return d


ALL_PHASES = ("p1", "p2", "p3", "p4a", "p4b", "p4c", "p4d", "p4e")


def kernel(**inputs):
    base = [_prep_core_inputs(inputs, c) for c in range(8)]
    if MODE == "fused":
        nc = build_program(phases=ALL_PHASES, test_mode=False)
        res = run_bass_kernel_spmd(nc, base, core_ids=list(range(8)))
        return np.stack([r["outT"].reshape(D, T).T for r in res.results]) \
            .astype(np.float32)
    scratch = [dict() for _ in range(8)]
    out = None
    for phn in ALL_PHASES:
        nc = build_program(phases=(phn,), test_mode=True)
        reads, writes = PHASE_IO[phn]
        ins = []
        for c in range(8):
            im = dict(base[c])
            for rname in reads:
                im[rname] = scratch[c][rname]
            ins.append(im)
        res = run_bass_kernel_spmd(nc, ins, core_ids=list(range(8)))
        for c in range(8):
            for wname in writes:
                scratch[c][wname] = res.results[c][wname]
        if phn == "p4e":
            out = np.stack([r["outT"].reshape(D, T).T for r in res.results])
    return out.astype(np.float32)
